# version 18
# baseline (speedup 1.0000x reference)
import numpy as np
import ml_dtypes
from contextlib import ExitStack

import concourse.bass as bass
import concourse.mybir as mybir
from concourse.bass_utils import run_bass_kernel_spmd

F32 = mybir.dt.float32
BF16 = mybir.dt.bfloat16
FP8 = mybir.dt.float8e4
U8 = mybir.dt.uint8
AF = mybir.ActivationFunctionType
ALU = mybir.AluOpType

B, S, D, H, DH = 4, 2048, 2048, 16, 128
KD = D // 128
TT = S // 2
ISQ = 1.0 / float(np.sqrt(np.float32(128.0)))
NSLAB1 = 24


def build_l1():
    nc = bass.Bass("TRN2", target_bir_lowering=False)
    xT = nc.dram_tensor("xT", [128, KD, S], F32, kind="ExternalInput").ap()
    wqk = nc.dram_tensor("wqk", [16, 128, KD * 128], F32, kind="ExternalInput").ap()
    wv = nc.dram_tensor("wv", [8, 128, KD * 128], F32, kind="ExternalInput").ap()
    vals_o = nc.dram_tensor("vals", [8, 128, S], BF16, kind="ExternalOutput").ap()

    with ExitStack() as ctx:
        ent = ctx.enter_context
        ones_bf = ent(nc.sbuf_tensor("ones_bf", [128, 128], BF16))
        xs = ent(nc.sbuf_tensor("xs", [128, KD * S], BF16))
        wpool = [ent(nc.sbuf_tensor(f"wp{i}", [128, KD * 128], BF16)) for i in range(4)]
        qk = [ent(nc.sbuf_tensor(f"qk{m}", [128, S], BF16)) for m in range(16)]
        vv = [ent(nc.sbuf_tensor(f"vv{m}", [128, S], BF16)) for m in range(8)]
        Eb = ent(nc.sbuf_tensor("Eb", [128, S], BF16))
        Zt = ent(nc.sbuf_tensor("Zt", [128, 1], F32))
        rf = ent(nc.sbuf_tensor("rf", [128, 1], F32))
        rb = ent(nc.sbuf_tensor("rb", [128, 1], BF16))
        cs_sb = ent(nc.sbuf_tensor("cs_sb", [128, 512], BF16))
        csrow = ent(nc.sbuf_tensor("csrow", [1, S], BF16))

        ps_proj = ent(nc.psum_tensor("ps_proj", [128, 2048], F32))
        ps_cs = ent(nc.psum_tensor("ps_cs", [128, 512], F32))
        ps_bc = ent(nc.psum_tensor("ps_bc", [128, 512], F32))

        ldx = ent(nc.semaphore("ldx"))
        ldx2 = ent(nc.semaphore("ldx2"))
        lw = [ent(nc.semaphore(f"lw{i}")) for i in range(4)]
        tp = ent(nc.semaphore("tp"))
        ve = ent(nc.semaphore("ve"))
        ts = ent(nc.semaphore("ts"))
        se = ent(nc.semaphore("se"))
        sr = ent(nc.semaphore("sr"))
        tcl = ent(nc.semaphore("tcl"))
        vcs = ent(nc.semaphore("vcs"))
        stg = ent(nc.semaphore("stg"))
        tbc = ent(nc.semaphore("tbc"))
        vb = ent(nc.semaphore("vb"))
        stv = ent(nc.semaphore("stv"))
        oset = ent(nc.semaphore("oset"))
        block = ent(nc.Block())

        @block.gpsimd
        def _(g):
            g.memset(ones_bf[:], 1.0).then_inc(oset, 1)
            g.dma_start(out=xs[:], in_=xT).then_inc(ldx, 16)
            for m in range(NSLAB1):
                if m >= 4:
                    g.wait_ge(lw[m % 4], 16 * (m // 4))
                    g.wait_ge(tp, 4 * (m - 3))
                src = wqk[m] if m < 16 else wv[m - 16]
                g.dma_start(out=wpool[m % 4][:], in_=src).then_inc(lw[m % 4], 16)

        @block.tensor
        def _(t):
            t.wait_ge(ldx, 16)
            for m in range(NSLAB1):
                t.wait_ge(lw[m % 4], 16 * (m // 4 + 1))
                for tc in range(4):
                    if m >= 1:
                        t.wait_ge(ve, 4 * (m - 1) + tc + 1)
                    for kd in range(KD):
                        mm = t.matmul(ps_proj[:, tc * 512:(tc + 1) * 512],
                                      wpool[m % 4][:, kd * 128:(kd + 1) * 128],
                                      xs[:, kd * S + tc * 512: kd * S + tc * 512 + 512],
                                      start=(kd == 0), stop=(kd == KD - 1))
                        if kd == KD - 1:
                            mm.then_inc(tp, 1)
            t.wait_ge(ve, 4 * NSLAB1)
            for h in range(8):
                for qt in range(16):
                    n = h * 16 + qt
                    t.wait_ge(se, n)
                    for kc in range(4):
                        t.matmul(ps_proj[:, kc * 512:(kc + 1) * 512],
                                 qk[h][:, qt * 128:(qt + 1) * 128],
                                 qk[8 + h][:, kc * 512:(kc + 1) * 512],
                                 start=True, stop=True).then_inc(ts, 1)
                    t.wait_ge(se, n + 1)
                    t.wait_ge(sr, n + 1)
                    if qt == 0:
                        t.wait_ge(vcs, h)
                    for j in range(4):
                        t.matmul(ps_cs[32 * j:32 * j + 1, 0:512],
                                 rb[:, 0:1], Eb[:, j * 512:(j + 1) * 512],
                                 start=(qt == 0), stop=(qt == 15),
                                 tile_position=(0, 32 * j)).then_inc(tcl, 1)
                if h == 0:
                    t.wait_ge(oset, 1)
                t.wait_ge(stg, 16 * (h + 1))
                for tc in range(4):
                    t.wait_ge(vb, h * 4 + tc)
                    t.matmul(ps_bc[:, 0:512], ones_bf[0:1, 0:128],
                             csrow[0:1, tc * 512:(tc + 1) * 512],
                             start=True, stop=True).then_inc(tbc, 1)

        @block.scalar
        def _(s):
            for h in range(8):
                for qt in range(16):
                    n = h * 16 + qt
                    s.wait_ge(ts, 4 * (n + 1))
                    s.wait_ge(tcl, 4 * n)
                    s.activation(Eb[:, 0:S], ps_proj[:, 0:2048], AF.Exp,
                                 bias=0.0, scale=ISQ, accum_out=Zt[:]).then_inc(se, 1)

        @block.vector
        def _(v):
            for m in range(NSLAB1):
                dst = qk[m] if m < 16 else vv[m - 16]
                for tc in range(4):
                    v.wait_ge(tp, m * 4 + tc + 1)
                    v.tensor_copy(dst[:, tc * 512:(tc + 1) * 512],
                                  ps_proj[:, tc * 512:(tc + 1) * 512]).then_inc(ve, 1)
            for h in range(8):
                for qt in range(16):
                    n = h * 16 + qt
                    v.wait_ge(se, n + 1)
                    with nc.allow_low_precision("1/Z stationary is bf16"):
                        v.reciprocal(rb[:, n % 2:n % 2 + 1],
                                     Zt[:, n % 2:n % 2 + 1]).then_inc(sr, 1)
                v.wait_ge(tcl, 64 * (h + 1))
                for j in range(4):
                    cp = v.tensor_copy(cs_sb[32 * j:32 * j + 1, 0:512],
                                       ps_cs[32 * j:32 * j + 1, 0:512])
                    if j == 3:
                        cp.then_inc(vcs, 1)
                for tc in range(4):
                    v.wait_ge(tbc, h * 4 + tc + 1)
                    v.tensor_tensor(vv[h][:, tc * 512:(tc + 1) * 512],
                                    vv[h][:, tc * 512:(tc + 1) * 512],
                                    ps_bc[:, 0:512], ALU.mult).then_inc(vb, 1)

        @block.sync
        def _(sy):
            def w8_load(lo, hi):
                for s in range(lo, hi):
                    if s >= 4:
                        sy.wait_ge(lw8[s % 4], 16 * (s // 4))
                        chp, op = divmod(s - 4, KD)
                        sy.wait_ge(tp, 192 + chp * 16 + op + 1)
                    sy.dma_start(out=w8p[s % 4][:],
                                 in_=w8o[s % KD]).then_inc(lw8[s % 4], 16)
            for h in range(8):
                sy.wait_ge(stg, 16 * h)
                sy.wait_ge(vcs, h + 1)
                sy.dma_start(out=csrow[0:1, :], in_=cs_sb[0:128:32, 0:512]).then_inc(stg, 16)
                sy.wait_ge(stv, 16 * h)
                sy.wait_ge(vb, 4 * (h + 1))
                sy.dma_start(out=vals_o[h], in_=vv[h][:]).then_inc(stv, 16)
            sy.wait_ge(stg, 16 * 8)
            sy.wait_ge(stv, 16 * 8)
    return nc


def build_l2():
    nc = bass.Bass("TRN2", target_bir_lowering=False)
    vals_i = nc.dram_tensor("vals16", [128, H, TT], BF16, kind="ExternalInput").ap()
    xTm = nc.dram_tensor("xTm", [128, KD, TT], F32, kind="ExternalInput").ap()
    wo = nc.dram_tensor("wo", [16, 128, KD * 128], F32, kind="ExternalInput").ap()
    w8 = nc.dram_tensor("w8", [128, KD * KD * 128], FP8, kind="ExternalInput").ap()
    s_in = nc.dram_tensor("s_in", [1, 1], BF16, kind="ExternalInput").ap()
    w_in = nc.dram_tensor("w_in", [1, 1], F32, kind="ExternalInput").ap()
    out_o = nc.dram_tensor("outT", [KD, 128, TT], F32, kind="ExternalOutput").ap()

    with ExitStack() as ctx:
        ent = ctx.enter_context
        ones_bf = ent(nc.sbuf_tensor("ones_bf", [128, 128], BF16))
        ones_f = ent(nc.sbuf_tensor("ones_f", [1, 128], F32))
        vsb = ent(nc.sbuf_tensor("vsb", [128, H * TT], BF16))
        xm = ent(nc.sbuf_tensor("xm", [128, KD * TT], BF16))
        res = ent(nc.sbuf_tensor("res", [128, KD * TT], BF16))
        w8s = ent(nc.sbuf_tensor("w8s", [128, KD * KD * 128], FP8))
        wpool = [ent(nc.sbuf_tensor(f"wp{i}", [128, KD * 128], BF16)) for i in range(4)]
        t1 = ent(nc.sbuf_tensor("t1", [128, TT], F32))
        d8t = ent(nc.sbuf_tensor("d8t", [128, TT], FP8))
        h8t = ent(nc.sbuf_tensor("h8t", [128, TT], FP8))
        hbt = ent(nc.sbuf_tensor("hbt", [128, TT], BF16))
        dbt = ent(nc.sbuf_tensor("dbt", [128, TT], BF16))
        msk = ent(nc.sbuf_tensor("msk", [128, TT], U8))
        mskf = ent(nc.sbuf_tensor("mskf", [128, TT], F32))
        stats_sb = ent(nc.sbuf_tensor("stats_sb", [128, 512], F32))
        row_sb = ent(nc.sbuf_tensor("row_sb", [1, 2048], F32))
        murow = ent(nc.sbuf_tensor("murow", [1, TT], F32))
        e2row = ent(nc.sbuf_tensor("e2row", [1, TT], F32))
        sdrow = ent(nc.sbuf_tensor("sdrow", [1, TT], F32))
        rrow = ent(nc.sbuf_tensor("rrow", [1, TT], F32))
        nrow = ent(nc.sbuf_tensor("nrow", [1, TT], F32))
        s_f = ent(nc.sbuf_tensor("s_f", [1, 1], F32))
        srow = ent(nc.sbuf_tensor("srow", [1, 1], F32))
        swrow = ent(nc.sbuf_tensor("swrow", [1, 1], F32))
        rs_sb = ent(nc.sbuf_tensor("rs_sb", [128, 1], F32))
        sw_sb = ent(nc.sbuf_tensor("sw_sb", [128, 1], F32))
        tmpn = ent(nc.sbuf_tensor("tmpn", [128, 512], F32))
        outst = [ent(nc.sbuf_tensor(f"outst{i}", [128, TT], F32)) for i in range(2)]

        ps_mm = ent(nc.psum_tensor("ps_mm", [128, 1024], F32))
        ps_st = ent(nc.psum_tensor("ps_st", [128, 512], F32))
        ps_b1 = ent(nc.psum_tensor("ps_b1", [128, 1024], F32))
        ps_b2 = ent(nc.psum_tensor("ps_b2", [128, 1024], F32))
        ps_sc = ent(nc.psum_tensor("ps_sc", [128, 2], F32))

        lbig = ent(nc.semaphore("lbig"))
        lw = [ent(nc.semaphore(f"lw{i}")) for i in range(4)]
        lds = ent(nc.semaphore("lds"))
        vpre = ent(nc.semaphore("vpre"))
        tsc = ent(nc.semaphore("tsc"))
        tmm = ent(nc.semaphore("tmm"))
        vev = ent(nc.semaphore("vev"))
        ssq = ent(nc.semaphore("ssq"))
        tst = ent(nc.semaphore("tst"))
        vst = ent(nc.semaphore("vst"))
        stg = ent(nc.semaphore("stg"))
        srw = ent(nc.semaphore("srw"))
        ssd = ent(nc.semaphore("ssd"))
        tbc = ent(nc.semaphore("tbc"))
        vn = ent(nc.semaphore("vn"))
        vq = ent(nc.semaphore("vq"))
        vrr = ent(nc.semaphore("vrr"))
        vch = ent(nc.semaphore("vch"))
        sto = ent(nc.semaphore("sto"))
        oset = ent(nc.semaphore("oset"))

        block = ent(nc.Block())

        @block.gpsimd
        def _(g):
            g.memset(ones_bf[:], 1.0).then_inc(oset, 1)
            g.memset(ones_f[:], 1.0).then_inc(oset, 1)
            g.dma_start(out=s_f[:], in_=s_in).then_inc(lds, 16)
            g.dma_start(out=swrow[:], in_=w_in).then_inc(lds, 16)
            g.dma_start(out=vsb[:], in_=vals_i).then_inc(lbig, 16)
            g.dma_start(out=xm[:], in_=xTm).then_inc(lbig, 16)
            g.dma_start(out=w8s[:], in_=w8).then_inc(lbig, 16)
            for m in range(16):
                if m >= 4:
                    g.wait_ge(lw[m % 4], 16 * (m // 4))
                    g.wait_ge(tmm, 2 * (m - 3))
                g.dma_start(out=wpool[m % 4][:], in_=wo[m]).then_inc(lw[m % 4], 16)

        @block.vector
        def _(v):
            vc = [0]
            def VI(instr):
                vc[0] += 1
                instr.then_inc(vch, 1)
            def VBAR():
                v.wait_ge(vch, vc[0])
            v.wait_ge(lds, 32)
            v.reciprocal(srow[:], s_f[:]).then_inc(vpre, 1)
            v.tensor_tensor(swrow[:], swrow[:], s_f[:], ALU.mult).then_inc(vpre, 1)
            v.wait_ge(tsc, 2)
            v.tensor_copy(rs_sb[:], ps_sc[:, 0:1]).then_inc(vpre, 1)
            v.tensor_copy(sw_sb[:], ps_sc[:, 1:2]).then_inc(vpre, 1)
            for m in range(KD):
                for tc in range(2):
                    v.wait_ge(tmm, m * 2 + tc + 1)
                    sl = slice(m * TT + tc * 512, m * TT + tc * 512 + 512)
                    v.tensor_tensor(res[:, sl], ps_mm[:, tc * 512:(tc + 1) * 512],
                                    xm[:, sl], ALU.add).then_inc(vev, 1)
            v.wait_ge(tst, 64)
            for j in range(4):
                cp = v.tensor_copy(stats_sb[32 * j:32 * j + 1, 0:512],
                                   ps_st[32 * j:32 * j + 1, 0:512])
                if j == 3:
                    cp.then_inc(vst, 1)
            v.wait_ge(stg, 16)
            VI(v.tensor_scalar(murow[:], row_sb[0:1, 0:1024], 1.0 / 2048.0, None, ALU.mult))
            VI(v.tensor_scalar(e2row[:], row_sb[0:1, 1024:2048], 1.0 / 2048.0, None, ALU.mult))
            VBAR()
            VI(v.tensor_tensor(sdrow[:], murow[:], murow[:], ALU.mult))
            VBAR()
            VI(v.tensor_tensor(sdrow[:], e2row[:], sdrow[:], ALU.subtract))
            VBAR()
            v.tensor_scalar(sdrow[:], sdrow[:], 1e-5, None, ALU.add).then_inc(srw, 1)
            v.wait_ge(ssd, 1)
            VI(v.reciprocal(rrow[:], sdrow[:]))
            VBAR()
            VI(v.tensor_tensor(nrow[:], murow[:], rrow[:], ALU.mult))
            VBAR()
            v.tensor_scalar(nrow[:], nrow[:], -1.0, None, ALU.mult).then_inc(srw, 1)
            v.wait_ge(tbc, 4)
            nadds = 0
            for tc in range(2):
                for m in range(KD):
                    sl = slice(m * TT + tc * 512, m * TT + tc * 512 + 512)
                    bs = slice(tc * 512, tc * 512 + 512)
                    if nadds:
                        v.wait_ge(vn, nadds)
                    VI(v.tensor_tensor(tmpn[:], res[:, sl], ps_b1[:, bs], ALU.mult))
                    VBAR()
                    v.tensor_tensor(xm[:, sl], tmpn[:], ps_b2[:, bs], ALU.add).then_inc(vn, 1)
                    nadds += 1
            v.wait_ge(vpre, 4)
            v.wait_ge(vn, 32)
            for m in range(KD):
                sl = slice(m * TT, (m + 1) * TT)
                VI(v.tensor_scalar(t1[:], xm[:, sl], rs_sb[:], -448.0, ALU.mult, ALU.max))
                VBAR()
                VI(v.tensor_scalar(d8t[:], t1[:], 448.0, None, ALU.min))
                VI(v.tensor_scalar(h8t[:], t1[:], 448.0, 0.5, ALU.min, ALU.mult))
                VI(v.tensor_tensor(mskf[:], t1[:], t1[:], ALU.mult))
                VBAR()
                VI(v.tensor_copy(dbt[:], d8t[:]))
                VI(v.tensor_scalar(hbt[:], h8t[:], 2.0, None, ALU.mult))
                VI(v.tensor_scalar(msk[:], mskf[:], 50176.0, None, ALU.is_ge))
                VBAR()
                v.select(vsb[:, sl], msk[:], hbt[:], dbt[:], add_drain=True).then_inc(vq, 1)
            for o in range(KD):
                for tc in range(2):
                    v.wait_ge(tmm, 32 + o * 2 + tc + 1)
                    sl = slice(o * TT + tc * 512, o * TT + tc * 512 + 512)
                    v.scalar_tensor_tensor(res[:, sl], ps_mm[:, tc * 512:(tc + 1) * 512],
                                           sw_sb[:], xm[:, sl],
                                           ALU.mult, ALU.add).then_inc(vev, 1)
            v.wait_ge(tst, 128)
            for j in range(4):
                cp = v.tensor_copy(stats_sb[32 * j:32 * j + 1, 0:512],
                                   ps_st[32 * j:32 * j + 1, 0:512])
                if j == 3:
                    cp.then_inc(vst, 1)
            v.wait_ge(stg, 32)
            VI(v.tensor_scalar(murow[:], row_sb[0:1, 0:1024], 1.0 / 2048.0, None, ALU.mult))
            VI(v.tensor_scalar(e2row[:], row_sb[0:1, 1024:2048], 1.0 / 2048.0, None, ALU.mult))
            VBAR()
            VI(v.tensor_tensor(sdrow[:], murow[:], murow[:], ALU.mult))
            VBAR()
            VI(v.tensor_tensor(sdrow[:], e2row[:], sdrow[:], ALU.subtract))
            VBAR()
            v.tensor_scalar(sdrow[:], sdrow[:], 1e-5, None, ALU.add).then_inc(srw, 1)
            v.wait_ge(ssd, 2)
            VI(v.reciprocal(rrow[:], sdrow[:]))
            VBAR()
            VI(v.tensor_tensor(nrow[:], murow[:], rrow[:], ALU.mult))
            VBAR()
            v.tensor_scalar(nrow[:], nrow[:], -1.0, None, ALU.mult).then_inc(srw, 1)
            v.wait_ge(tbc, 8)
            for m in range(KD):
                ob = outst[m % 2]
                if m >= 2:
                    v.wait_ge(sto, 16 * (m - 1))
                if m >= 1:
                    v.wait_ge(vn, 32 + m)
                for tc in range(2):
                    sl = slice(m * TT + tc * 512, m * TT + tc * 512 + 512)
                    bs = slice(tc * 512, tc * 512 + 512)
                    VI(v.tensor_tensor(tmpn[:], res[:, sl], ps_b1[:, bs], ALU.mult))
                    VBAR()
                    ad = v.tensor_tensor(ob[:, tc * 512:(tc + 1) * 512], tmpn[:],
                                         ps_b2[:, bs], ALU.add)
                    if tc == 0:
                        VI(ad)
                        VBAR()
                    else:
                        ad.then_inc(vn, 1)

        @block.tensor
        def _(t):
            t.wait_ge(oset, 2)
            t.wait_ge(vpre, 2)
            t.matmul(ps_sc[:, 0:1], ones_f[0:1, 0:128], srow[0:1, 0:1],
                     start=True, stop=True).then_inc(tsc, 1)
            t.matmul(ps_sc[:, 1:2], ones_f[0:1, 0:128], swrow[0:1, 0:1],
                     start=True, stop=True).then_inc(tsc, 1)
            t.wait_ge(lbig, 16 * 3)
            for m in range(KD):
                t.wait_ge(lw[m % 4], 16 * (m // 4 + 1))
                for tc in range(2):
                    if m >= 1:
                        t.wait_ge(vev, 2 * (m - 1) + tc + 1)
                    for kh in range(H):
                        mm = t.matmul(ps_mm[:, tc * 512:(tc + 1) * 512],
                                      wpool[m % 4][:, kh * 128:(kh + 1) * 128],
                                      vsb[:, kh * TT + tc * 512: kh * TT + tc * 512 + 512],
                                      start=(kh == 0), stop=(kh == H - 1))
                        if kh == H - 1:
                            mm.then_inc(tmm, 1)
            t.wait_ge(vev, 32)
            t.wait_ge(ssq, 16)
            for m in range(KD):
                for j in range(4):
                    src = res if j < 2 else vsb
                    tc = j % 2
                    sl = slice(m * TT + tc * 512, m * TT + tc * 512 + 512)
                    t.matmul(ps_st[32 * j:32 * j + 1, 0:512],
                             ones_bf[0:128, 0:1], src[:, sl],
                             start=(m == 0), stop=(m == KD - 1),
                             tile_position=(0, 32 * j)).then_inc(tst, 1)
            t.wait_ge(srw, 2)
            for tc in range(2):
                bs = slice(tc * 512, tc * 512 + 512)
                t.matmul(ps_b1[:, bs], ones_f[0:1, 0:128],
                         rrow[0:1, tc * 512:(tc + 1) * 512], start=True, stop=True).then_inc(tbc, 1)
                t.matmul(ps_b2[:, bs], ones_f[0:1, 0:128],
                         nrow[0:1, tc * 512:(tc + 1) * 512], start=True, stop=True).then_inc(tbc, 1)
            for o in range(KD):
                for tc in range(2):
                    t.wait_ge(vev, 32 + (2 * (o - 1) + tc + 1 if o >= 1 else 0))
                    for ki in range(KD):
                        t.wait_ge(vq, ki + 1)
                        mm = t.matmul(ps_mm[:, tc * 512:(tc + 1) * 512],
                                      w8s[:, (ki * KD + o) * 128:(ki * KD + o) * 128 + 128],
                                      vsb[:, ki * TT + tc * 512: ki * TT + tc * 512 + 512],
                                      start=(ki == 0), stop=(ki == KD - 1))
                        if ki == KD - 1:
                            mm.then_inc(tmm, 1)
            t.wait_ge(vev, 64)
            t.wait_ge(ssq, 32)
            for m in range(KD):
                for j in range(4):
                    src = res if j < 2 else vsb
                    tc = j % 2
                    sl = slice(m * TT + tc * 512, m * TT + tc * 512 + 512)
                    t.matmul(ps_st[32 * j:32 * j + 1, 0:512],
                             ones_bf[0:128, 0:1], src[:, sl],
                             start=(m == 0), stop=(m == KD - 1),
                             tile_position=(0, 32 * j)).then_inc(tst, 1)
            t.wait_ge(srw, 4)
            t.wait_ge(vn, 32)
            for tc in range(2):
                bs = slice(tc * 512, tc * 512 + 512)
                t.matmul(ps_b1[:, bs], ones_f[0:1, 0:128],
                         rrow[0:1, tc * 512:(tc + 1) * 512], start=True, stop=True).then_inc(tbc, 1)
                t.matmul(ps_b2[:, bs], ones_f[0:1, 0:128],
                         nrow[0:1, tc * 512:(tc + 1) * 512], start=True, stop=True).then_inc(tbc, 1)

        @block.scalar
        def _(s):
            s.wait_ge(vev, 32)
            for m in range(KD):
                sl = slice(m * TT, (m + 1) * TT)
                s.activation(vsb[:, sl], res[:, sl], AF.Square,
                             bias=0.0, scale=1.0).then_inc(ssq, 1)
            s.wait_ge(srw, 1)
            s.activation(sdrow[:], sdrow[:], AF.Sqrt, bias=0.0, scale=1.0).then_inc(ssd, 1)
            s.wait_ge(vev, 64)
            for m in range(KD):
                sl = slice(m * TT, (m + 1) * TT)
                s.activation(vsb[:, sl], res[:, sl], AF.Square,
                             bias=0.0, scale=1.0).then_inc(ssq, 1)
            s.wait_ge(srw, 3)
            s.activation(sdrow[:], sdrow[:], AF.Sqrt, bias=0.0, scale=1.0).then_inc(ssd, 1)

        @block.sync
        def _(sy):
            def w8_load(lo, hi):
                for s in range(lo, hi):
                    if s >= 4:
                        sy.wait_ge(lw8[s % 4], 16 * (s // 4))
                        chp, op = divmod(s - 4, KD)
                        sy.wait_ge(tp, 192 + chp * 16 + op + 1)
                    sy.dma_start(out=w8p[s % 4][:],
                                 in_=w8o[s % KD]).then_inc(lw8[s % 4], 16)
            sy.wait_ge(vst, 1)
            sy.dma_start(out=row_sb[0:1, :], in_=stats_sb[0:128:32, 0:512]).then_inc(stg, 16)
            sy.wait_ge(stg, 16)
            sy.wait_ge(vst, 2)
            sy.dma_start(out=row_sb[0:1, :], in_=stats_sb[0:128:32, 0:512]).then_inc(stg, 16)
            for m in range(KD):
                sy.wait_ge(sto, 16 * m)
                sy.wait_ge(vn, 32 + m + 1)
                sy.dma_start(out=out_o[m], in_=outst[m % 2][:]).then_inc(sto, 16)
            sy.wait_ge(sto, 16 * KD)
    return nc




NH = 16
CT = 512


def build_fused():
    nc = bass.Bass("TRN2", target_bir_lowering=False)
    xT = nc.dram_tensor("xT", [128, KD, S], F32, kind="ExternalInput").ap()
    wqk = nc.dram_tensor("wqk", [2 * NH, 128, KD * 128], F32, kind="ExternalInput").ap()
    wv = nc.dram_tensor("wv", [NH, 128, KD * 128], F32, kind="ExternalInput").ap()
    wo = nc.dram_tensor("wo", [KD, 128, KD * 128], F32, kind="ExternalInput").ap()
    w8o = nc.dram_tensor("w8o", [KD, 128, KD * 128], FP8, kind="ExternalInput").ap()
    s_in = nc.dram_tensor("s_in", [1, 1], BF16, kind="ExternalInput").ap()
    w_in = nc.dram_tensor("w_in", [1, 1], F32, kind="ExternalInput").ap()
    out_o = nc.dram_tensor("outT", [KD, 128, TT], F32, kind="ExternalOutput").ap()

    with ExitStack() as ctx:
        ent = ctx.enter_context
        ones_bf = ent(nc.sbuf_tensor("ones_bf", [128, 128], BF16))
        ones_f = ent(nc.sbuf_tensor("ones_f", [1, 128], F32))
        xs = ent(nc.sbuf_tensor("xs", [128, KD * S], BF16))
        qkp = [ent(nc.sbuf_tensor(f"qkp{i}", [128, S], BF16)) for i in range(4)]
        wpool = [ent(nc.sbuf_tensor(f"wp{i}", [128, KD * 128], BF16)) for i in range(4)]
        w8p = [ent(nc.sbuf_tensor(f"w8p{i}", [128, KD * 128], FP8)) for i in range(4)]
        Ebs = [ent(nc.sbuf_tensor(f"Eb{i}", [128, S], BF16)) for i in range(2)]
        Zt = ent(nc.sbuf_tensor("Zt", [128, 4], F32))
        ztmp = ent(nc.sbuf_tensor("ztmp", [128, 2], F32))
        rb = ent(nc.sbuf_tensor("rb", [128, 2], BF16))
        cs_sb = ent(nc.sbuf_tensor("cs_sb", [128, 512], BF16))
        cs16 = ent(nc.sbuf_tensor("cs16", [NH, 2 * CT], BF16))
        csrow = ent(nc.sbuf_tensor("csrow", [1, NH * CT], BF16))
        vv = [ent(nc.sbuf_tensor(f"vv{m}", [128, CT], BF16)) for m in range(KD)]
        res = ent(nc.sbuf_tensor("res", [128, KD * CT], BF16))
        t1 = [ent(nc.sbuf_tensor(f"t1_{i}", [128, CT], F32)) for i in range(2)]
        d8t = [ent(nc.sbuf_tensor(f"d8t{i}", [128, CT], FP8)) for i in range(2)]
        h8t = [ent(nc.sbuf_tensor(f"h8t{i}", [128, CT], FP8)) for i in range(2)]
        hbt = [ent(nc.sbuf_tensor(f"hbt{i}", [128, CT], BF16)) for i in range(2)]
        dbt = [ent(nc.sbuf_tensor(f"dbt{i}", [128, CT], BF16)) for i in range(2)]
        msk = [ent(nc.sbuf_tensor(f"msk{i}", [128, CT], U8)) for i in range(2)]
        mskf = [ent(nc.sbuf_tensor(f"mskf{i}", [128, CT], F32)) for i in range(2)]
        tmpn = [ent(nc.sbuf_tensor(f"tmpn{i}", [128, CT], F32)) for i in range(2)]
        outst = [ent(nc.sbuf_tensor(f"outst{i}", [128, CT], F32)) for i in range(4)]
        stats_sb = ent(nc.sbuf_tensor("stats_sb", [128, 512], F32))
        row_sb = ent(nc.sbuf_tensor("row_sb", [1, 2 * CT], F32))
        murow = ent(nc.sbuf_tensor("murow", [1, CT], F32))
        e2row = ent(nc.sbuf_tensor("e2row", [1, CT], F32))
        sdrow = ent(nc.sbuf_tensor("sdrow", [1, CT], F32))
        rrow = ent(nc.sbuf_tensor("rrow", [1, CT], F32))
        nrow = ent(nc.sbuf_tensor("nrow", [1, CT], F32))
        s_f = ent(nc.sbuf_tensor("s_f", [1, 1], F32))
        srow = ent(nc.sbuf_tensor("srow", [1, 1], F32))
        swrow = ent(nc.sbuf_tensor("swrow", [1, 1], F32))
        rs_sb = ent(nc.sbuf_tensor("rs_sb", [128, 1], F32))
        sw_sb = ent(nc.sbuf_tensor("sw_sb", [128, 1], F32))

        ps_acc = ent(nc.psum_tensor("ps_acc", [128, 1024], F32))
        ps_sc = ent(nc.psum_tensor("ps_sc", [128, 2048], F32))
        ps_cs = ent(nc.psum_tensor("ps_cs", [128, 512], F32))
        ps_bb = ent(nc.psum_tensor("ps_bb", [128, 2], F32))

        ldx = ent(nc.semaphore("ldx"))
        ldx2 = ent(nc.semaphore("ldx2"))
        lds = ent(nc.semaphore("lds"))
        lw = [ent(nc.semaphore(f"lw{i}")) for i in range(4)]
        lw8 = [ent(nc.semaphore(f"lw8{i}")) for i in range(4)]
        oset = ent(nc.semaphore("oset"))
        vpre = ent(nc.semaphore("vpre"))
        tsc = ent(nc.semaphore("tsc"))
        tp = ent(nc.semaphore("tp"))
        cve = ent(nc.semaphore("cve"))
        ts = ent(nc.semaphore("ts"))
        se = ent(nc.semaphore("se"))
        sr = ent(nc.semaphore("sr"))
        tcl = ent(nc.semaphore("tcl"))
        vcs = ent(nc.semaphore("vcs"))
        stgh = ent(nc.semaphore("stgh"))
        stgc = ent(nc.semaphore("stgc"))
        tbc = ent(nc.semaphore("tbc"))
        vmu = ent(nc.semaphore("vmu"))
        tst = ent(nc.semaphore("tst"))
        vst = ent(nc.semaphore("vst"))
        stgr = ent(nc.semaphore("stgr"))
        srw = ent(nc.semaphore("srw"))
        ssd = ent(nc.semaphore("ssd"))
        ssq = ent(nc.semaphore("ssq"))
        vn = ent(nc.semaphore("vn"))
        vq = ent(nc.semaphore("vq"))
        vqd = ent(nc.semaphore("vqd"))
        sq8 = ent(nc.semaphore("sq8"))
        sto = ent(nc.semaphore("sto"))
        vch = ent(nc.semaphore("vch"))
        block = ent(nc.Block())

        qk_groups = [("qk", s, tc) for s in range(2 * NH) for tc in range(4)]
        sched = {}
        sched[-1] = [(g, hf) for g in range(8) for hf in (0, 1)]
        gidx = 8
        for h in range(NH):
            for qt in range(16):
                sched[(h, qt)] = []
            if h < NH - 1:
                for qt in range(16):
                    sched[(h, qt)] = [(gidx + qt // 2, qt % 2)]
                gidx += 8
        assert gidx == 128
        for qt in range(16):
            sched[(15, qt)] = [(128 + 2 * qt, 0), (128 + 2 * qt, 1)] if False else \
                              [(128 + qt, 0), (128 + qt, 1)]

        GTAB = [("v", 0), ("wo", 0), ("v", 1), ("wo", 1), ("ffn", 0), ("ffn", 1)]

        def group_info(g):
            pg = g - 128
            kind, ch = GTAB[pg // 16]
            return kind, ch, pg % 16

        def slab_of_group(g):
            if g < 128:
                return g // 4
            kind, ch, idx = group_info(g)
            if kind == "v":
                return 32 + ch * 32 + idx
            if kind == "wo":
                return 48 + ch * 32 + idx
            return None
        n_groups_total = 128 + 6 * 16
        group_end_of_slab = {}
        for g in range(n_groups_total):
            sl_ = slab_of_group(g)
            if sl_ is not None:
                group_end_of_slab[sl_] = g + 1

        @block.gpsimd
        def _(g_):
            g_.memset(ones_bf[:], 1.0).then_inc(oset, 1)
            g_.memset(ones_f[:], 1.0).then_inc(oset, 1)
            g_.dma_start(out=s_f[:], in_=s_in).then_inc(lds, 16)
            g_.dma_start(out=swrow[:], in_=w_in).then_inc(lds, 16)
            g_.dma_start(out=xs[:, 0:8 * S], in_=xT[:, 0:8, :]).then_inc(ldx, 16)
            g_.dma_start(out=xs[:, 8 * S:KD * S], in_=xT[:, 8:KD, :]).then_inc(ldx2, 16)
            stream = [("w", s, wqk[s]) for s in range(2 * NH)]
            for ch in range(2):
                stream += [("w", 32 + ch * 32 + i, wv[i]) for i in range(NH)]
                stream += [("w", 48 + ch * 32 + i, wo[i]) for i in range(KD)]
            for kind, s, src_ap in stream:
                if s >= 4:
                    g_.wait_ge(lw[s % 4], 16 * (s // 4))
                    g_.wait_ge(tp, group_end_of_slab[s - 4])
                g_.dma_start(out=wpool[s % 4][:], in_=src_ap).then_inc(lw[s % 4], 16)

        vvB = [qkp[r // 4][:, (r % 4) * 512:(r % 4) * 512 + 512] for r in range(KD)]
        resB = [xs[:, m * S + S - 512: m * S + S] for m in range(KD)]
        sqB = [xs[:, m * S + 2 * CT: m * S + 3 * CT] for m in range(KD)]

        @block.tensor
        def _(t):
            t.wait_ge(oset, 2)
            t.wait_ge(vpre, 2)
            t.matmul(ps_bb[:, 0:1], ones_f[0:1, 0:128], srow[0:1, 0:1],
                     start=True, stop=True).then_inc(tsc, 1)
            t.matmul(ps_bb[:, 1:2], ones_f[0:1, 0:128], swrow[0:1, 0:1],
                     start=True, stop=True).then_inc(tsc, 1)

            def emit_group_half(g, hf):
                bank = g % 2
                osl = slice(bank * 512, bank * 512 + 512)
                kds = range(8 * hf, 8 * hf + 8)
                if hf == 0:
                    if g >= 2:
                        t.wait_ge(cve, g - 1)
                if g < 128:
                    s, tc = g // 4, g % 4
                    if hf == 0:
                        t.wait_ge(lw[s % 4], 16 * (s // 4 + 1))
                    if g < 8:
                        t.wait_ge(ldx if hf == 0 else ldx2, 16)
                    for kd in kds:
                        mm = t.matmul(ps_acc[:, osl], wpool[s % 4][:, kd * 128:(kd + 1) * 128],
                                      xs[:, kd * S + tc * 512: kd * S + tc * 512 + 512],
                                      start=(kd == 0), stop=(kd == KD - 1),
                                      skip_group_check=True)
                        if kd == KD - 1:
                            mm.then_inc(tp, 1)
                    return
                kind, ch, idx = group_info(g)
                if kind == "v":
                    s = 32 + ch * 32 + idx
                    if hf == 0:
                        t.wait_ge(lw[s % 4], 16 * (s // 4 + 1))
                    for kd in kds:
                        mm = t.matmul(ps_acc[:, osl], wpool[s % 4][:, kd * 128:(kd + 1) * 128],
                                      xs[:, kd * S + ch * CT: kd * S + ch * CT + CT],
                                      start=(kd == 0), stop=(kd == KD - 1),
                                      skip_group_check=True)
                        if kd == KD - 1:
                            mm.then_inc(tp, 1)
                    return
                if kind == "wo":
                    s = 48 + ch * 32 + idx
                    vsrc = vv if ch == 0 else vvB
                    if hf == 0:
                        t.wait_ge(lw[s % 4], 16 * (s // 4 + 1))
                        t.wait_ge(vmu, (ch + 1) * NH)
                    for kh in kds:
                        mm = t.matmul(ps_acc[:, osl], wpool[s % 4][:, kh * 128:(kh + 1) * 128],
                                      vsrc[kh][:, 0:CT] if ch == 0 else vsrc[kh],
                                      start=(kh == 0), stop=(kh == NH - 1),
                                      skip_group_check=True)
                        if kh == NH - 1:
                            mm.then_inc(tp, 1)
                    return
                j = ch * KD + idx
                if hf == 0:
                    t.wait_ge(lw8[j % 4], 16 * (j // 4 + 1))
                    t.wait_ge(vq, (ch + 1) * KD)
                for ki in kds:
                    mm = t.matmul(ps_acc[:, osl], w8p[j % 4][:, ki * 128:(ki + 1) * 128],
                                  vv[ki][:, 0:CT],
                                  start=(ki == 0), stop=(ki == KD - 1),
                                  skip_group_check=True)
                    if ki == KD - 1:
                        mm.then_inc(tp, 1)

            def emit_group(g):
                emit_group_half(g, 0)
                emit_group_half(g, 1)

            ng = 0
            for g, hf in sched[-1]:
                emit_group_half(g, hf)
            def emit_colsum(h, m):
                n = h * 16 + m
                t.wait_ge(sr, n + 1)
                if m == 0:
                    t.wait_ge(vcs, h)
                for j in range(4):
                    mm = t.matmul(ps_cs[32 * j:32 * j + 1, 0:256],
                                  rb[:, m % 2:m % 2 + 1], Ebs[m % 2][:, j * 256:(j + 1) * 256],
                                  start=(m == 0), stop=(m == 15),
                                  tile_position=(0, 32 * j))
                    if j == 3:
                        mm.then_inc(tcl, 1)

            for h in range(NH):
                for qt in range(16):
                    n = h * 16 + qt
                    if qt == 0:
                        t.wait_ge(cve, 8 * (h + 1))
                    t.wait_ge(se, max(0, 2 * n - 1))
                    for kc in range(2):
                        mm = t.matmul(ps_sc[:, kc * 512:(kc + 1) * 512],
                                      qkp[(2 * h) % 4][:, qt * 128:(qt + 1) * 128],
                                      qkp[(2 * h + 1) % 4][:, kc * 512:(kc + 1) * 512],
                                      start=True, stop=True)
                        if kc == 1:
                            mm.then_inc(ts, 1)
                    for g, hf in sched[(h, qt)]:
                        emit_group_half(g, hf)
                    t.wait_ge(se, 2 * n)
                    for kc in range(2, 4):
                        mm = t.matmul(ps_sc[:, kc * 512:(kc + 1) * 512],
                                      qkp[(2 * h) % 4][:, qt * 128:(qt + 1) * 128],
                                      qkp[(2 * h + 1) % 4][:, kc * 512:(kc + 1) * 512],
                                      start=True, stop=True)
                        if kc == 3:
                            mm.then_inc(ts, 1)
                    if qt >= 1:
                        emit_colsum(h, qt - 1)
                emit_colsum(h, 15)
            t.wait_ge(stgc, 16)
            t.wait_ge(se, 512)
            for hh in range(NH):
                bs = slice((hh % 2) * 512, (hh % 2) * 512 + 512)
                if hh >= 2:
                    t.wait_ge(vmu, hh - 1)
                t.matmul(ps_sc[:, bs], ones_bf[0:1, 0:128],
                         csrow[0:1, hh * CT:(hh + 1) * CT],
                         start=True, stop=True).then_inc(tbc, 1)
            for i in range(16):
                emit_group(144 + i)
            for m in range(KD):
                t.wait_ge(cve, 144 + m + 1)
                t.wait_ge(ssq, m + 1)
                t.matmul(ps_cs[0:1, 0:512], ones_bf[0:128, 0:1],
                         res[:, m * CT:(m + 1) * CT],
                         start=(m == 0), stop=(m == KD - 1),
                         tile_position=(0, 0)).then_inc(tst, 1)
                t.matmul(ps_cs[32:33, 0:512], ones_bf[0:128, 0:1],
                         vv[m][:, 0:CT],
                         start=(m == 0), stop=(m == KD - 1),
                         tile_position=(0, 32)).then_inc(tst, 1)
            t.wait_ge(srw, 2)
            t.matmul(ps_sc[:, 512:1024], ones_f[0:1, 0:128], rrow[0:1, 0:CT],
                     start=True, stop=True).then_inc(tbc, 1)
            t.matmul(ps_sc[:, 1024:1536], ones_f[0:1, 0:128], nrow[0:1, 0:CT],
                     start=True, stop=True).then_inc(tbc, 1)
            for i in range(16):
                emit_group(160 + i)
            t.wait_ge(stgc, 32)
            for hh in range(NH):
                bs = slice(0, 512) if hh % 2 == 0 else slice(1536, 2048)
                t.wait_ge(vmu, NH + hh - 1)
                t.matmul(ps_sc[:, bs], ones_bf[0:1, 0:128],
                         csrow[0:1, hh * CT:(hh + 1) * CT],
                         start=True, stop=True).then_inc(tbc, 1)
            for i in range(16):
                emit_group(176 + i)
            for i in range(16):
                emit_group(192 + i)
            t.wait_ge(vst, 1)
            for m in range(KD):
                t.wait_ge(cve, 192 + m + 1)
                t.wait_ge(ssq, 2 * KD + m + 1)
                t.matmul(ps_cs[0:1, 0:512], ones_bf[0:128, 0:1],
                         res[:, m * CT:(m + 1) * CT],
                         start=(m == 0), stop=(m == KD - 1),
                         tile_position=(0, 0)).then_inc(tst, 1)
                t.matmul(ps_cs[32:33, 0:512], ones_bf[0:128, 0:1],
                         vv[m][:, 0:CT],
                         start=(m == 0), stop=(m == KD - 1),
                         tile_position=(0, 32)).then_inc(tst, 1)
            t.wait_ge(srw, 4)
            t.wait_ge(vn, KD)
            t.matmul(ps_sc[:, 512:1024], ones_f[0:1, 0:128], rrow[0:1, 0:CT],
                     start=True, stop=True).then_inc(tbc, 1)
            t.matmul(ps_sc[:, 1024:1536], ones_f[0:1, 0:128], nrow[0:1, 0:CT],
                     start=True, stop=True).then_inc(tbc, 1)
            t.wait_ge(vst, 2)
            for m in range(KD):
                t.wait_ge(cve, 176 + m + 1)
                t.wait_ge(ssq, KD + m + 1)
                t.matmul(ps_cs[0:1, 0:512], ones_bf[0:128, 0:1],
                         resB[m], start=(m == 0), stop=(m == KD - 1),
                         tile_position=(0, 0)).then_inc(tst, 1)
                t.matmul(ps_cs[32:33, 0:512], ones_bf[0:128, 0:1],
                         sqB[m], start=(m == 0), stop=(m == KD - 1),
                         tile_position=(0, 32)).then_inc(tst, 1)
            t.wait_ge(srw, 6)
            t.wait_ge(vmu, 2 * NH)
            t.matmul(ps_sc[:, 0:512], ones_f[0:1, 0:128], rrow[0:1, 0:CT],
                     start=True, stop=True).then_inc(tbc, 1)
            t.matmul(ps_sc[:, 1536:2048], ones_f[0:1, 0:128], nrow[0:1, 0:CT],
                     start=True, stop=True).then_inc(tbc, 1)
            for i in range(16):
                emit_group(208 + i)
            t.wait_ge(vst, 3)
            for m in range(KD):
                t.wait_ge(cve, 208 + m + 1)
                t.wait_ge(ssq, 3 * KD + m + 1)
                t.matmul(ps_cs[0:1, 0:512], ones_bf[0:128, 0:1],
                         res[:, m * CT:(m + 1) * CT],
                         start=(m == 0), stop=(m == KD - 1),
                         tile_position=(0, 0)).then_inc(tst, 1)
                t.matmul(ps_cs[32:33, 0:512], ones_bf[0:128, 0:1],
                         vv[m][:, 0:CT],
                         start=(m == 0), stop=(m == KD - 1),
                         tile_position=(0, 32)).then_inc(tst, 1)
            t.wait_ge(srw, 8)
            t.wait_ge(vn, 3 * KD)
            t.matmul(ps_sc[:, 512:1024], ones_f[0:1, 0:128], rrow[0:1, 0:CT],
                     start=True, stop=True).then_inc(tbc, 1)
            t.matmul(ps_sc[:, 1024:1536], ones_f[0:1, 0:128], nrow[0:1, 0:CT],
                     start=True, stop=True).then_inc(tbc, 1)

        @block.scalar
        def _(s_):
            for h in range(NH):
                for qt in range(16):
                    n = h * 16 + qt
                    s_.wait_ge(ts, 2 * n + 1)
                    if n >= 1:
                        s_.wait_ge(tcl, n - 1)
                    p = n % 2
                    s_.activation(Ebs[p][:, 0:1024], ps_sc[:, 0:1024], AF.Exp,
                                  bias=0.0, scale=ISQ,
                                  accum_out=Zt[:, 2 * p:2 * p + 1]).then_inc(se, 1)
                    s_.wait_ge(ts, 2 * n + 2)
                    s_.activation(Ebs[p][:, 1024:2048], ps_sc[:, 1024:2048], AF.Exp,
                                  bias=0.0, scale=ISQ,
                                  accum_out=Zt[:, 2 * p + 1:2 * p + 2]).then_inc(se, 1)
            def qdq_helpers(ch):
                for p in range(KD // 2):
                    pp = ch * KD // 2 + p
                    if pp >= 1:
                        s_.wait_ge(vq, 2 * pp)
                    for j in range(2):
                        s_.wait_ge(vqd, 4 * pp + 2 * j + 2)
                        s_.activation(d8t[j][:], t1[j][:], AF.Copy,
                                      bias=0.0, scale=1.0).then_inc(sq8, 1)
                        s_.wait_ge(sq8, 8 * pp + 4 * j + 1)
                        s_.activation(dbt[j][:], d8t[j][:], AF.Copy,
                                      bias=0.0, scale=1.0).then_inc(sq8, 1)
                        s_.activation(hbt[j][:], h8t[j][:], AF.Copy,
                                      bias=0.0, scale=2.0).then_inc(sq8, 1)
                        s_.activation(mskf[j][:], h8t[j][:], AF.Square,
                                      bias=0.0, scale=2.0).then_inc(sq8, 1)

            def squares(cbase, srcs, dsts, tpw=0, tstw=0):
                if tpw:
                    s_.wait_ge(tp, tpw)
                if tstw:
                    s_.wait_ge(tst, tstw)
                for m in range(KD):
                    s_.wait_ge(cve, cbase + m + 1)
                    s_.activation(dsts(m), srcs(m), AF.Square,
                                  bias=0.0, scale=1.0).then_inc(ssq, 1)

            def sqrt_stage(k):
                s_.wait_ge(srw, 2 * k + 1)
                s_.activation(sdrow[:], sdrow[:], AF.Sqrt,
                              bias=0.0, scale=1.0).then_inc(ssd, 1)

            res_m = lambda m: res[:, m * CT:(m + 1) * CT]
            vv_m = lambda m: vv[m][:, 0:CT]
            squares(144, res_m, vv_m, tpw=160)
            sqrt_stage(0)
            qdq_helpers(0)
            squares(176, lambda m: resB[m], lambda m: sqB[m])
            squares(192, res_m, vv_m, tpw=208, tstw=32)
            sqrt_stage(1)
            sqrt_stage(2)
            qdq_helpers(1)
            squares(208, res_m, vv_m, tpw=224, tstw=96)
            sqrt_stage(3)

        @block.vector
        def _(v):
            vc = [0]
            def VI(instr):
                vc[0] += 1
                instr.then_inc(vch, 1)
            def VBAR():
                v.wait_ge(vch, vc[0])
            v.wait_ge(lds, 32)
            v.reciprocal(srow[:], s_f[:]).then_inc(vpre, 1)
            v.tensor_tensor(swrow[:], swrow[:], s_f[:], ALU.mult).then_inc(vpre, 1)
            v.wait_ge(tsc, 2)
            v.tensor_copy(rs_sb[:], ps_bb[:, 0:1]).then_inc(vpre, 1)
            v.tensor_copy(sw_sb[:], ps_bb[:, 1:2]).then_inc(vpre, 1)

            def consume_group(g):
                bank = g % 2
                osl = slice(bank * 512, bank * 512 + 512)
                v.wait_ge(tp, g + 1)
                if g < 128:
                    s, tc = g // 4, g % 4
                    if s >= 4:
                        v.wait_ge(ts, 32 * ((s - 4) // 2 + 1))
                    v.tensor_copy(qkp[s % 4][:, tc * 512:(tc + 1) * 512],
                                  ps_acc[:, osl]).then_inc(cve, 1)
                    return
                kind, ch, idx = group_info(g)
                if kind == "v":
                    dst = vv[idx][:, 0:CT] if ch == 0 else vvB[idx]
                    v.tensor_copy(dst, ps_acc[:, osl]).then_inc(cve, 1)
                    return
                if kind == "wo":
                    m = idx
                    sl = slice(m * S + ch * CT, m * S + ch * CT + CT)
                    dst = res[:, m * CT:(m + 1) * CT] if ch == 0 else resB[m]
                    v.tensor_tensor(dst, ps_acc[:, osl],
                                    xs[:, sl], ALU.add).then_inc(cve, 1)
                    return
                o = idx
                xnsl = slice(o * S + ch * CT, o * S + ch * CT + CT)
                v.scalar_tensor_tensor(res[:, o * CT:(o + 1) * CT], ps_acc[:, osl],
                                       sw_sb[:], xs[:, xnsl],
                                       ALU.mult, ALU.add).then_inc(cve, 1)

            for g, hf in sched[-1]:
                if hf == 1:
                    consume_group(g)
            for h in range(NH):
                for qt in range(16):
                    n = h * 16 + qt
                    for g, hf in sched[(h, qt)]:
                        if hf == 1:
                            consume_group(g)
                    v.wait_ge(se, 2 * (n + 1))
                    p = n % 2
                    VI(v.tensor_tensor(ztmp[:, p:p + 1], Zt[:, 2 * p:2 * p + 1],
                                       Zt[:, 2 * p + 1:2 * p + 2], ALU.add))
                    VBAR()
                    with nc.allow_low_precision("1/Z stationary is bf16"):
                        v.reciprocal(rb[:, p:p + 1],
                                     ztmp[:, p:p + 1]).then_inc(sr, 1)
                v.wait_ge(tcl, 16 * (h + 1))
                v.wait_ge(stgh, 16 * h)
                for j in range(4):
                    cp = v.tensor_copy(cs_sb[32 * j:32 * j + 1, 0:256],
                                       ps_cs[32 * j:32 * j + 1, 0:256])
                if True:
                    cp.then_inc(vcs, 1)
            def rows_stage(k, tbc_guard=0):
                v.wait_ge(tst, 32 * (k + 1))
                v.tensor_copy(stats_sb[0:1, 0:512], ps_cs[0:1, 0:512])
                v.tensor_copy(stats_sb[32:33, 0:512], ps_cs[32:33, 0:512]).then_inc(vst, 1)
                if tbc_guard:
                    v.wait_ge(tbc, tbc_guard)
                v.wait_ge(stgr, 16 * (k + 1))
                VI(v.tensor_scalar(murow[:], row_sb[0:1, 0:CT], 1.0 / 2048.0, None, ALU.mult))
                VI(v.tensor_scalar(e2row[:], row_sb[0:1, CT:2 * CT], 1.0 / 2048.0, None, ALU.mult))
                VBAR()
                VI(v.tensor_tensor(sdrow[:], murow[:], murow[:], ALU.mult))
                VBAR()
                VI(v.tensor_tensor(sdrow[:], e2row[:], sdrow[:], ALU.subtract))
                VBAR()
                v.tensor_scalar(sdrow[:], sdrow[:], 1e-5, None, ALU.add).then_inc(srw, 1)
                v.wait_ge(ssd, k + 1)
                VI(v.reciprocal(rrow[:], sdrow[:]))
                VBAR()
                VI(v.tensor_tensor(nrow[:], murow[:], rrow[:], ALU.mult))
                VBAR()
                v.tensor_scalar(nrow[:], nrow[:], -1.0, None, ALU.mult).then_inc(srw, 1)

            def normalize(srcs, dsts, tbc_need, vn_base, rsl=slice(512, 1024),
                          nsl=slice(1024, 1536)):
                v.wait_ge(tbc, tbc_need)
                for m in range(KD):
                    tb = tmpn[m % 2]
                    if m >= 2:
                        v.wait_ge(vn, vn_base + m - 1)
                    VI(v.tensor_tensor(tb[:], srcs(m), ps_sc[:, rsl], ALU.mult))
                    VBAR()
                    v.tensor_tensor(dsts(m), tb[:], ps_sc[:, nsl],
                                    ALU.add).then_inc(vn, 1)

            def qdq(ch, vn_need):
                v.wait_ge(vpre, 4)
                v.wait_ge(vn, vn_need)
                for p in range(KD // 2):
                    pp = ch * KD // 2 + p
                    if pp >= 1:
                        v.wait_ge(sq8, 8 * pp)
                    for j in range(2):
                        m = 2 * p + j
                        sl = slice(m * S + ch * CT, m * S + ch * CT + CT)
                        v.tensor_scalar(t1[j][:], xs[:, sl], rs_sb[:], -448.0,
                                        ALU.mult, ALU.max).then_inc(vqd, 1)
                        v.wait_ge(vqd, 4 * pp + 2 * j + 1)
                        v.tensor_scalar(h8t[j][:], t1[j][:], 448.0, 0.5,
                                        ALU.min, ALU.mult).then_inc(vqd, 1)
                    v.wait_ge(sq8, 8 * pp + 8)
                    for j in range(2):
                        m = 2 * p + j
                        VI(v.tensor_scalar(msk[j][:], mskf[j][:], 57600.0, None, ALU.is_ge))
                        VBAR()
                        v.select(vv[m][:, 0:CT], msk[j][:], hbt[j][:], dbt[j][:],
                                 add_drain=True).then_inc(vq, 1)

            for hh in range(NH):
                bs = slice((hh % 2) * 512, (hh % 2) * 512 + 512)
                v.wait_ge(tbc, hh + 1)
                v.tensor_tensor(vv[hh][:, 0:CT], vv[hh][:, 0:CT],
                                ps_sc[:, bs], ALU.mult).then_inc(vmu, 1)
            for i in range(16):
                consume_group(144 + i)
            rows_stage(0)
            normalize(lambda m: res[:, m * CT:(m + 1) * CT],
                      lambda m: xs[:, m * S: m * S + CT], 18, 0)
            for i in range(16):
                consume_group(160 + i)
            for hh in range(NH):
                bs = slice(0, 512) if hh % 2 == 0 else slice(1536, 2048)
                v.wait_ge(tbc, 19 + hh)
                v.tensor_tensor(vvB[hh], vvB[hh], ps_sc[:, bs], ALU.mult).then_inc(vmu, 1)
            qdq(0, KD)
            for i in range(16):
                consume_group(176 + i)
            for i in range(16):
                consume_group(192 + i)
            rows_stage(1)
            rows_stage(2, tbc_guard=36)
            normalize(lambda m: resB[m],
                      lambda m: xs[:, m * S + CT: m * S + 2 * CT], 38, KD,
                      rsl=slice(0, 512), nsl=slice(1536, 2048))
            qdq(1, 2 * KD)
            v.wait_ge(tbc, 36)
            for m in range(KD):
                ob = outst[m % 4]
                tb = tmpn[m % 2]
                if m >= 4:
                    v.wait_ge(sto, 16 * (m - 3))
                if m >= 2:
                    v.wait_ge(vn, 2 * KD + m - 1)
                VI(v.tensor_tensor(tb[:], res[:, m * CT:(m + 1) * CT],
                                   ps_sc[:, 512:1024], ALU.mult))
                VBAR()
                v.tensor_tensor(ob[:, 0:CT], tb[:], ps_sc[:, 1024:1536],
                                ALU.add).then_inc(vn, 1)
            for i in range(16):
                consume_group(208 + i)
            rows_stage(3)
            v.wait_ge(tbc, 40)
            for m in range(KD):
                ob = outst[m % 4]
                tb = tmpn[m % 2]
                v.wait_ge(sto, 16 * (KD + m - 3))
                if m >= 2:
                    v.wait_ge(vn, 3 * KD + m - 1)
                VI(v.tensor_tensor(tb[:], res[:, m * CT:(m + 1) * CT],
                                   ps_sc[:, 512:1024], ALU.mult))
                VBAR()
                v.tensor_tensor(ob[:, 0:CT], tb[:], ps_sc[:, 1024:1536],
                                ALU.add).then_inc(vn, 1)

        @block.sync
        def _(sy):
            def w8_load(lo, hi):
                for s in range(lo, hi):
                    if s >= 4:
                        sy.wait_ge(lw8[s % 4], 16 * (s // 4))
                        chp, op = divmod(s - 4, KD)
                        sy.wait_ge(tp, 192 + chp * 16 + op + 1)
                    sy.dma_start(out=w8p[s % 4][:],
                                 in_=w8o[s % KD]).then_inc(lw8[s % 4], 16)
            for h in range(NH):
                sy.wait_ge(stgh, 16 * h)
                sy.wait_ge(vcs, h + 1)
                sy.dma_start(out=cs16[h:h + 1, 0:2 * CT],
                             in_=cs_sb[0:128:32, 0:256]).then_inc(stgh, 16)
            sy.wait_ge(stgh, 16 * NH)
            sy.dma_start(out=csrow[0:1, :],
                         in_=cs16[0:NH, 0:CT]).then_inc(stgc, 16)
            sy.wait_ge(stgc, 16)
            sy.wait_ge(tbc, 16)
            sy.dma_start(out=csrow[0:1, :],
                         in_=cs16[0:NH, CT:2 * CT]).then_inc(stgc, 16)
            w8_load(0, 4)
            sy.wait_ge(stgr, 0)
            sy.wait_ge(vst, 1)
            sy.dma_start(out=row_sb[0:1, :],
                         in_=stats_sb[0:64:32, 0:512]).then_inc(stgr, 16)
            w8_load(4, 16)
            sy.wait_ge(stgr, 16)
            sy.wait_ge(vst, 2)
            sy.dma_start(out=row_sb[0:1, :],
                         in_=stats_sb[0:64:32, 0:512]).then_inc(stgr, 16)
            w8_load(16, 20)
            sy.wait_ge(stgr, 32)
            sy.wait_ge(vst, 3)
            sy.dma_start(out=row_sb[0:1, :],
                         in_=stats_sb[0:64:32, 0:512]).then_inc(stgr, 16)
            for m in range(KD):
                sy.wait_ge(sto, 16 * m)
                sy.wait_ge(vn, 2 * KD + m + 1)
                sy.dma_start(out=out_o[m][:, 0:CT],
                             in_=outst[m % 4][:]).then_inc(sto, 16)
            w8_load(20, 32)
            sy.wait_ge(stgr, 48)
            sy.wait_ge(vst, 4)
            sy.dma_start(out=row_sb[0:1, :],
                         in_=stats_sb[0:64:32, 0:512]).then_inc(stgr, 16)
            for m in range(KD):
                sy.wait_ge(sto, 16 * (KD + m))
                sy.wait_ge(vn, 3 * KD + m + 1)
                sy.dma_start(out=out_o[m][:, CT:2 * CT],
                             in_=outst[m % 4][:]).then_inc(sto, 16)
            sy.wait_ge(sto, 16 * 2 * KD)
            sy.wait_ge(stgc, 32)
    return nc


_cache = {}


def _get(name, builder):
    if name not in _cache:
        _cache[name] = builder()
    return _cache[name]


FUSED = True


def kernel(L0_attn_x, L0_attn_w_qkv, L0_attn_w_o, L0_ffn_weight,
           L0_ffn_weight_scale, L0_ffn_input_scale):
    x = np.asarray(L0_attn_x)
    wqkv = np.asarray(L0_attn_w_qkv)
    w_o = np.asarray(L0_attn_w_o)
    w8 = np.asarray(L0_ffn_weight)
    wsc = np.asarray(L0_ffn_weight_scale, dtype=np.float32).reshape(1, 1)
    isc = np.asarray(L0_ffn_input_scale).reshape(1, 1)

    def tile_slabs(A):
        ncol = A.shape[1]
        return np.ascontiguousarray(
            A.reshape(KD, 128, ncol // 128, 128).transpose(2, 1, 0, 3)
        ).reshape(ncol // 128, 128, KD * 128)

    if FUSED:
        wqk32 = tile_slabs(np.concatenate(
            [c for h in range(H)
             for c in (wqkv[:, h * 128:(h + 1) * 128],
                       wqkv[:, 2048 + h * 128: 2048 + (h + 1) * 128])], axis=1))
        wv16 = tile_slabs(wqkv[:, 4096:6144])
        wo_t = tile_slabs(w_o)
        w8o = tile_slabs(np.ascontiguousarray(w8.T))
        xTr = []
        for c in range(8):
            b, g = c // 2, c % 2
            xb = x[b].T
            rot = np.concatenate([xb[:, g * TT:(g + 1) * TT],
                                  xb[:, (1 - g) * TT:(2 - g) * TT]], axis=1)
            xTr.append(np.ascontiguousarray(
                rot.reshape(KD, 128, S).transpose(1, 0, 2)))
        nc = _get("fused", build_fused)
        ins = [dict(xT=xTr[c], wqk=wqk32, wv=wv16, wo=wo_t, w8o=w8o,
                    s_in=isc, w_in=wsc) for c in range(8)]
        r = run_bass_kernel_spmd(nc, ins, core_ids=list(range(8)))
        out = np.empty((B, S, D), dtype=np.float32)
        for c in range(8):
            b, g = c // 2, c % 2
            oT = np.asarray(r.results[c]["outT"])
            out[b, g * TT:(g + 1) * TT, :] = oT.transpose(2, 0, 1).reshape(TT, D)
        return out

    xT2 = [np.ascontiguousarray(x[b].T.reshape(KD, 128, S).transpose(1, 0, 2))
           for b in range(B)]
    wqk_t = []
    wv_t = []
    for g in range(2):
        qcols = wqkv[:, g * 1024:(g + 1) * 1024]
        kcols = wqkv[:, 2048 + g * 1024: 2048 + (g + 1) * 1024]
        vcols = wqkv[:, 4096 + g * 1024: 4096 + (g + 1) * 1024]
        wqk_t.append(tile_slabs(np.concatenate([qcols, kcols], axis=1)))
        wv_t.append(tile_slabs(vcols))

    nc1 = _get("l1", build_l1)
    in1 = [dict(xT=xT2[c // 2], wqk=wqk_t[c % 2], wv=wv_t[c % 2]) for c in range(8)]
    res1 = run_bass_kernel_spmd(nc1, in1, core_ids=list(range(8)))
    vals = [np.asarray(res1.results[c]["vals"]) for c in range(8)]

    vals16 = []
    for c in range(8):
        b, g = c // 2, c % 2
        lo, hi = g * TT, (g + 1) * TT
        v16 = np.concatenate([vals[2 * b][:, :, lo:hi], vals[2 * b + 1][:, :, lo:hi]],
                             axis=0)
        vals16.append(np.ascontiguousarray(v16.transpose(1, 0, 2)))

    wo_t = tile_slabs(w_o)
    w8T = np.ascontiguousarray(
        w8.T.reshape(KD, 128, KD, 128).transpose(1, 0, 2, 3)
    ).reshape(128, KD * KD * 128)
    xTm = [np.ascontiguousarray(xT2[b][:, :, g * TT:(g + 1) * TT])
           for b in range(B) for g in range(2)]

    nc2 = _get("l2", build_l2)
    in2 = [dict(vals16=vals16[c], xTm=xTm[c], wo=wo_t, w8=w8T,
                s_in=isc, w_in=wsc) for c in range(8)]
    res2 = run_bass_kernel_spmd(nc2, in2, core_ids=list(range(8)))

    out = np.empty((B, S, D), dtype=np.float32)
    for c in range(8):
        b, g = c // 2, c % 2
        oT = np.asarray(res2.results[c]["outT"])
        out[b, g * TT:(g + 1) * TT, :] = oT.transpose(2, 0, 1).reshape(TT, D)
    return out


# version 19
# speedup vs baseline: 1.0212x; 1.0212x over previous
import numpy as np
import ml_dtypes
from contextlib import ExitStack

import concourse.bass as bass
import concourse.mybir as mybir
from concourse.bass_utils import run_bass_kernel_spmd

F32 = mybir.dt.float32
BF16 = mybir.dt.bfloat16
FP8 = mybir.dt.float8e4
U8 = mybir.dt.uint8
AF = mybir.ActivationFunctionType
ALU = mybir.AluOpType

B, S, D, H, DH = 4, 2048, 2048, 16, 128
KD = D // 128
TT = S // 2
ISQ = 1.0 / float(np.sqrt(np.float32(128.0)))
NSLAB1 = 24


def build_l1():
    nc = bass.Bass("TRN2", target_bir_lowering=False)
    xT = nc.dram_tensor("xT", [128, KD, S], F32, kind="ExternalInput").ap()
    wqk = nc.dram_tensor("wqk", [16, 128, KD * 128], F32, kind="ExternalInput").ap()
    wv = nc.dram_tensor("wv", [8, 128, KD * 128], F32, kind="ExternalInput").ap()
    vals_o = nc.dram_tensor("vals", [8, 128, S], BF16, kind="ExternalOutput").ap()

    with ExitStack() as ctx:
        ent = ctx.enter_context
        ones_bf = ent(nc.sbuf_tensor("ones_bf", [128, 128], BF16))
        xs = ent(nc.sbuf_tensor("xs", [128, KD * S], BF16))
        wpool = [ent(nc.sbuf_tensor(f"wp{i}", [128, KD * 128], BF16)) for i in range(4)]
        qk = [ent(nc.sbuf_tensor(f"qk{m}", [128, S], BF16)) for m in range(16)]
        vv = [ent(nc.sbuf_tensor(f"vv{m}", [128, S], BF16)) for m in range(8)]
        Eb = ent(nc.sbuf_tensor("Eb", [128, S], BF16))
        Zt = ent(nc.sbuf_tensor("Zt", [128, 1], F32))
        rf = ent(nc.sbuf_tensor("rf", [128, 1], F32))
        rb = ent(nc.sbuf_tensor("rb", [128, 1], BF16))
        cs_sb = ent(nc.sbuf_tensor("cs_sb", [128, 512], BF16))
        csrow = ent(nc.sbuf_tensor("csrow", [1, S], BF16))

        ps_proj = ent(nc.psum_tensor("ps_proj", [128, 2048], F32))
        ps_cs = ent(nc.psum_tensor("ps_cs", [128, 512], F32))
        ps_bc = ent(nc.psum_tensor("ps_bc", [128, 512], F32))

        ldx = ent(nc.semaphore("ldx"))
        ldx2 = ent(nc.semaphore("ldx2"))
        lw = [ent(nc.semaphore(f"lw{i}")) for i in range(4)]
        tp = ent(nc.semaphore("tp"))
        ve = ent(nc.semaphore("ve"))
        ts = ent(nc.semaphore("ts"))
        se = ent(nc.semaphore("se"))
        sr = ent(nc.semaphore("sr"))
        tcl = ent(nc.semaphore("tcl"))
        vcs = ent(nc.semaphore("vcs"))
        stg = ent(nc.semaphore("stg"))
        tbc = ent(nc.semaphore("tbc"))
        vb = ent(nc.semaphore("vb"))
        stv = ent(nc.semaphore("stv"))
        oset = ent(nc.semaphore("oset"))
        block = ent(nc.Block())

        @block.gpsimd
        def _(g):
            g.memset(ones_bf[:], 1.0).then_inc(oset, 1)
            g.dma_start(out=xs[:], in_=xT).then_inc(ldx, 16)
            for m in range(NSLAB1):
                if m >= 4:
                    g.wait_ge(lw[m % 4], 16 * (m // 4))
                    g.wait_ge(tp, 4 * (m - 3))
                src = wqk[m] if m < 16 else wv[m - 16]
                g.dma_start(out=wpool[m % 4][:], in_=src).then_inc(lw[m % 4], 16)

        @block.tensor
        def _(t):
            t.wait_ge(ldx, 16)
            for m in range(NSLAB1):
                t.wait_ge(lw[m % 4], 16 * (m // 4 + 1))
                for tc in range(4):
                    if m >= 1:
                        t.wait_ge(ve, 4 * (m - 1) + tc + 1)
                    for kd in range(KD):
                        mm = t.matmul(ps_proj[:, tc * 512:(tc + 1) * 512],
                                      wpool[m % 4][:, kd * 128:(kd + 1) * 128],
                                      xs[:, kd * S + tc * 512: kd * S + tc * 512 + 512],
                                      start=(kd == 0), stop=(kd == KD - 1))
                        if kd == KD - 1:
                            mm.then_inc(tp, 1)
            t.wait_ge(ve, 4 * NSLAB1)
            for h in range(8):
                for qt in range(16):
                    n = h * 16 + qt
                    t.wait_ge(se, n)
                    for kc in range(4):
                        t.matmul(ps_proj[:, kc * 512:(kc + 1) * 512],
                                 qk[h][:, qt * 128:(qt + 1) * 128],
                                 qk[8 + h][:, kc * 512:(kc + 1) * 512],
                                 start=True, stop=True).then_inc(ts, 1)
                    t.wait_ge(se, n + 1)
                    t.wait_ge(sr, n + 1)
                    if qt == 0:
                        t.wait_ge(vcs, h)
                    for j in range(4):
                        t.matmul(ps_cs[32 * j:32 * j + 1, 0:512],
                                 rb[:, 0:1], Eb[:, j * 512:(j + 1) * 512],
                                 start=(qt == 0), stop=(qt == 15),
                                 tile_position=(0, 32 * j)).then_inc(tcl, 1)
                if h == 0:
                    t.wait_ge(oset, 1)
                t.wait_ge(stg, 16 * (h + 1))
                for tc in range(4):
                    t.wait_ge(vb, h * 4 + tc)
                    t.matmul(ps_bc[:, 0:512], ones_bf[0:1, 0:128],
                             csrow[0:1, tc * 512:(tc + 1) * 512],
                             start=True, stop=True).then_inc(tbc, 1)

        @block.scalar
        def _(s):
            for h in range(8):
                for qt in range(16):
                    n = h * 16 + qt
                    s.wait_ge(ts, 4 * (n + 1))
                    s.wait_ge(tcl, 4 * n)
                    s.activation(Eb[:, 0:S], ps_proj[:, 0:2048], AF.Exp,
                                 bias=0.0, scale=ISQ, accum_out=Zt[:]).then_inc(se, 1)

        @block.vector
        def _(v):
            for m in range(NSLAB1):
                dst = qk[m] if m < 16 else vv[m - 16]
                for tc in range(4):
                    v.wait_ge(tp, m * 4 + tc + 1)
                    v.tensor_copy(dst[:, tc * 512:(tc + 1) * 512],
                                  ps_proj[:, tc * 512:(tc + 1) * 512]).then_inc(ve, 1)
            for h in range(8):
                for qt in range(16):
                    n = h * 16 + qt
                    v.wait_ge(se, n + 1)
                    with nc.allow_low_precision("1/Z stationary is bf16"):
                        v.reciprocal(rb[:, n % 2:n % 2 + 1],
                                     Zt[:, n % 2:n % 2 + 1]).then_inc(sr, 1)
                v.wait_ge(tcl, 64 * (h + 1))
                for j in range(4):
                    cp = v.tensor_copy(cs_sb[32 * j:32 * j + 1, 0:512],
                                       ps_cs[32 * j:32 * j + 1, 0:512])
                    if j == 3:
                        cp.then_inc(vcs, 1)
                for tc in range(4):
                    v.wait_ge(tbc, h * 4 + tc + 1)
                    v.tensor_tensor(vv[h][:, tc * 512:(tc + 1) * 512],
                                    vv[h][:, tc * 512:(tc + 1) * 512],
                                    ps_bc[:, 0:512], ALU.mult).then_inc(vb, 1)

        @block.sync
        def _(sy):
            for h in range(8):
                sy.wait_ge(stg, 16 * h)
                sy.wait_ge(vcs, h + 1)
                sy.dma_start(out=csrow[0:1, :], in_=cs_sb[0:128:32, 0:512]).then_inc(stg, 16)
                sy.wait_ge(stv, 16 * h)
                sy.wait_ge(vb, 4 * (h + 1))
                sy.dma_start(out=vals_o[h], in_=vv[h][:]).then_inc(stv, 16)
            sy.wait_ge(stg, 16 * 8)
            sy.wait_ge(stv, 16 * 8)
    return nc


def build_l2():
    nc = bass.Bass("TRN2", target_bir_lowering=False)
    vals_i = nc.dram_tensor("vals16", [128, H, TT], BF16, kind="ExternalInput").ap()
    xTm = nc.dram_tensor("xTm", [128, KD, TT], F32, kind="ExternalInput").ap()
    wo = nc.dram_tensor("wo", [16, 128, KD * 128], F32, kind="ExternalInput").ap()
    w8 = nc.dram_tensor("w8", [128, KD * KD * 128], FP8, kind="ExternalInput").ap()
    s_in = nc.dram_tensor("s_in", [1, 1], BF16, kind="ExternalInput").ap()
    w_in = nc.dram_tensor("w_in", [1, 1], F32, kind="ExternalInput").ap()
    out_o = nc.dram_tensor("outT", [KD, 128, TT], F32, kind="ExternalOutput").ap()

    with ExitStack() as ctx:
        ent = ctx.enter_context
        ones_bf = ent(nc.sbuf_tensor("ones_bf", [128, 128], BF16))
        ones_f = ent(nc.sbuf_tensor("ones_f", [1, 128], F32))
        vsb = ent(nc.sbuf_tensor("vsb", [128, H * TT], BF16))
        xm = ent(nc.sbuf_tensor("xm", [128, KD * TT], BF16))
        res = ent(nc.sbuf_tensor("res", [128, KD * TT], BF16))
        w8s = ent(nc.sbuf_tensor("w8s", [128, KD * KD * 128], FP8))
        wpool = [ent(nc.sbuf_tensor(f"wp{i}", [128, KD * 128], BF16)) for i in range(4)]
        t1 = ent(nc.sbuf_tensor("t1", [128, TT], F32))
        d8t = ent(nc.sbuf_tensor("d8t", [128, TT], FP8))
        h8t = ent(nc.sbuf_tensor("h8t", [128, TT], FP8))
        hbt = ent(nc.sbuf_tensor("hbt", [128, TT], BF16))
        dbt = ent(nc.sbuf_tensor("dbt", [128, TT], BF16))
        msk = ent(nc.sbuf_tensor("msk", [128, TT], U8))
        mskf = ent(nc.sbuf_tensor("mskf", [128, TT], F32))
        stats_sb = ent(nc.sbuf_tensor("stats_sb", [128, 512], F32))
        row_sb = ent(nc.sbuf_tensor("row_sb", [1, 2048], F32))
        murow = ent(nc.sbuf_tensor("murow", [1, TT], F32))
        e2row = ent(nc.sbuf_tensor("e2row", [1, TT], F32))
        sdrow = ent(nc.sbuf_tensor("sdrow", [1, TT], F32))
        rrow = ent(nc.sbuf_tensor("rrow", [1, TT], F32))
        nrow = ent(nc.sbuf_tensor("nrow", [1, TT], F32))
        s_f = ent(nc.sbuf_tensor("s_f", [1, 1], F32))
        srow = ent(nc.sbuf_tensor("srow", [1, 1], F32))
        swrow = ent(nc.sbuf_tensor("swrow", [1, 1], F32))
        rs_sb = ent(nc.sbuf_tensor("rs_sb", [128, 1], F32))
        sw_sb = ent(nc.sbuf_tensor("sw_sb", [128, 1], F32))
        tmpn = ent(nc.sbuf_tensor("tmpn", [128, 512], F32))
        outst = [ent(nc.sbuf_tensor(f"outst{i}", [128, TT], F32)) for i in range(2)]

        ps_mm = ent(nc.psum_tensor("ps_mm", [128, 1024], F32))
        ps_st = ent(nc.psum_tensor("ps_st", [128, 512], F32))
        ps_b1 = ent(nc.psum_tensor("ps_b1", [128, 1024], F32))
        ps_b2 = ent(nc.psum_tensor("ps_b2", [128, 1024], F32))
        ps_sc = ent(nc.psum_tensor("ps_sc", [128, 2], F32))

        lbig = ent(nc.semaphore("lbig"))
        lw = [ent(nc.semaphore(f"lw{i}")) for i in range(4)]
        lds = ent(nc.semaphore("lds"))
        vpre = ent(nc.semaphore("vpre"))
        tsc = ent(nc.semaphore("tsc"))
        tmm = ent(nc.semaphore("tmm"))
        vev = ent(nc.semaphore("vev"))
        ssq = ent(nc.semaphore("ssq"))
        tst = ent(nc.semaphore("tst"))
        vst = ent(nc.semaphore("vst"))
        stg = ent(nc.semaphore("stg"))
        srw = ent(nc.semaphore("srw"))
        ssd = ent(nc.semaphore("ssd"))
        tbc = ent(nc.semaphore("tbc"))
        vn = ent(nc.semaphore("vn"))
        vq = ent(nc.semaphore("vq"))
        vrr = ent(nc.semaphore("vrr"))
        vch = ent(nc.semaphore("vch"))
        sto = ent(nc.semaphore("sto"))
        oset = ent(nc.semaphore("oset"))

        block = ent(nc.Block())

        @block.gpsimd
        def _(g):
            g.memset(ones_bf[:], 1.0).then_inc(oset, 1)
            g.memset(ones_f[:], 1.0).then_inc(oset, 1)
            g.dma_start(out=s_f[:], in_=s_in).then_inc(lds, 16)
            g.dma_start(out=swrow[:], in_=w_in).then_inc(lds, 16)
            g.dma_start(out=vsb[:], in_=vals_i).then_inc(lbig, 16)
            g.dma_start(out=xm[:], in_=xTm).then_inc(lbig, 16)
            g.dma_start(out=w8s[:], in_=w8).then_inc(lbig, 16)
            for m in range(16):
                if m >= 4:
                    g.wait_ge(lw[m % 4], 16 * (m // 4))
                    g.wait_ge(tmm, 2 * (m - 3))
                g.dma_start(out=wpool[m % 4][:], in_=wo[m]).then_inc(lw[m % 4], 16)

        @block.vector
        def _(v):
            vc = [0]
            def VI(instr):
                vc[0] += 1
                instr.then_inc(vch, 1)
            def VBAR():
                v.wait_ge(vch, vc[0])
            v.wait_ge(lds, 32)
            v.reciprocal(srow[:], s_f[:]).then_inc(vpre, 1)
            v.tensor_tensor(swrow[:], swrow[:], s_f[:], ALU.mult).then_inc(vpre, 1)
            v.wait_ge(tsc, 2)
            v.tensor_copy(rs_sb[:], ps_sc[:, 0:1]).then_inc(vpre, 1)
            v.tensor_copy(sw_sb[:], ps_sc[:, 1:2]).then_inc(vpre, 1)
            for m in range(KD):
                for tc in range(2):
                    v.wait_ge(tmm, m * 2 + tc + 1)
                    sl = slice(m * TT + tc * 512, m * TT + tc * 512 + 512)
                    v.tensor_tensor(res[:, sl], ps_mm[:, tc * 512:(tc + 1) * 512],
                                    xm[:, sl], ALU.add).then_inc(vev, 1)
            v.wait_ge(tst, 64)
            for j in range(4):
                cp = v.tensor_copy(stats_sb[32 * j:32 * j + 1, 0:512],
                                   ps_st[32 * j:32 * j + 1, 0:512])
                if j == 3:
                    cp.then_inc(vst, 1)
            v.wait_ge(stg, 16)
            VI(v.tensor_scalar(murow[:], row_sb[0:1, 0:1024], 1.0 / 2048.0, None, ALU.mult))
            VI(v.tensor_scalar(e2row[:], row_sb[0:1, 1024:2048], 1.0 / 2048.0, None, ALU.mult))
            VBAR()
            VI(v.tensor_tensor(sdrow[:], murow[:], murow[:], ALU.mult))
            VBAR()
            VI(v.tensor_tensor(sdrow[:], e2row[:], sdrow[:], ALU.subtract))
            VBAR()
            v.tensor_scalar(sdrow[:], sdrow[:], 1e-5, None, ALU.add).then_inc(srw, 1)
            v.wait_ge(ssd, 1)
            VI(v.reciprocal(rrow[:], sdrow[:]))
            VBAR()
            VI(v.tensor_tensor(nrow[:], murow[:], rrow[:], ALU.mult))
            VBAR()
            v.tensor_scalar(nrow[:], nrow[:], -1.0, None, ALU.mult).then_inc(srw, 1)
            v.wait_ge(tbc, 4)
            nadds = 0
            for tc in range(2):
                for m in range(KD):
                    sl = slice(m * TT + tc * 512, m * TT + tc * 512 + 512)
                    bs = slice(tc * 512, tc * 512 + 512)
                    if nadds:
                        v.wait_ge(vn, nadds)
                    VI(v.tensor_tensor(tmpn[:], res[:, sl], ps_b1[:, bs], ALU.mult))
                    VBAR()
                    v.tensor_tensor(xm[:, sl], tmpn[:], ps_b2[:, bs], ALU.add).then_inc(vn, 1)
                    nadds += 1
            v.wait_ge(vpre, 4)
            v.wait_ge(vn, 32)
            for m in range(KD):
                sl = slice(m * TT, (m + 1) * TT)
                VI(v.tensor_scalar(t1[:], xm[:, sl], rs_sb[:], -448.0, ALU.mult, ALU.max))
                VBAR()
                VI(v.tensor_scalar(d8t[:], t1[:], 448.0, None, ALU.min))
                VI(v.tensor_scalar(h8t[:], t1[:], 448.0, 0.5, ALU.min, ALU.mult))
                VI(v.tensor_tensor(mskf[:], t1[:], t1[:], ALU.mult))
                VBAR()
                VI(v.tensor_copy(dbt[:], d8t[:]))
                VI(v.tensor_scalar(hbt[:], h8t[:], 2.0, None, ALU.mult))
                VI(v.tensor_scalar(msk[:], mskf[:], 50176.0, None, ALU.is_ge))
                VBAR()
                v.select(vsb[:, sl], msk[:], hbt[:], dbt[:], add_drain=True).then_inc(vq, 1)
            for o in range(KD):
                for tc in range(2):
                    v.wait_ge(tmm, 32 + o * 2 + tc + 1)
                    sl = slice(o * TT + tc * 512, o * TT + tc * 512 + 512)
                    v.scalar_tensor_tensor(res[:, sl], ps_mm[:, tc * 512:(tc + 1) * 512],
                                           sw_sb[:], xm[:, sl],
                                           ALU.mult, ALU.add).then_inc(vev, 1)
            v.wait_ge(tst, 128)
            for j in range(4):
                cp = v.tensor_copy(stats_sb[32 * j:32 * j + 1, 0:512],
                                   ps_st[32 * j:32 * j + 1, 0:512])
                if j == 3:
                    cp.then_inc(vst, 1)
            v.wait_ge(stg, 32)
            VI(v.tensor_scalar(murow[:], row_sb[0:1, 0:1024], 1.0 / 2048.0, None, ALU.mult))
            VI(v.tensor_scalar(e2row[:], row_sb[0:1, 1024:2048], 1.0 / 2048.0, None, ALU.mult))
            VBAR()
            VI(v.tensor_tensor(sdrow[:], murow[:], murow[:], ALU.mult))
            VBAR()
            VI(v.tensor_tensor(sdrow[:], e2row[:], sdrow[:], ALU.subtract))
            VBAR()
            v.tensor_scalar(sdrow[:], sdrow[:], 1e-5, None, ALU.add).then_inc(srw, 1)
            v.wait_ge(ssd, 2)
            VI(v.reciprocal(rrow[:], sdrow[:]))
            VBAR()
            VI(v.tensor_tensor(nrow[:], murow[:], rrow[:], ALU.mult))
            VBAR()
            v.tensor_scalar(nrow[:], nrow[:], -1.0, None, ALU.mult).then_inc(srw, 1)
            v.wait_ge(tbc, 8)
            for m in range(KD):
                ob = outst[m % 2]
                if m >= 2:
                    v.wait_ge(sto, 16 * (m - 1))
                if m >= 1:
                    v.wait_ge(vn, 32 + m)
                for tc in range(2):
                    sl = slice(m * TT + tc * 512, m * TT + tc * 512 + 512)
                    bs = slice(tc * 512, tc * 512 + 512)
                    VI(v.tensor_tensor(tmpn[:], res[:, sl], ps_b1[:, bs], ALU.mult))
                    VBAR()
                    ad = v.tensor_tensor(ob[:, tc * 512:(tc + 1) * 512], tmpn[:],
                                         ps_b2[:, bs], ALU.add)
                    if tc == 0:
                        VI(ad)
                        VBAR()
                    else:
                        ad.then_inc(vn, 1)

        @block.tensor
        def _(t):
            t.wait_ge(oset, 2)
            t.wait_ge(vpre, 2)
            t.matmul(ps_sc[:, 0:1], ones_f[0:1, 0:128], srow[0:1, 0:1],
                     start=True, stop=True).then_inc(tsc, 1)
            t.matmul(ps_sc[:, 1:2], ones_f[0:1, 0:128], swrow[0:1, 0:1],
                     start=True, stop=True).then_inc(tsc, 1)
            t.wait_ge(lbig, 16 * 3)
            for m in range(KD):
                t.wait_ge(lw[m % 4], 16 * (m // 4 + 1))
                for tc in range(2):
                    if m >= 1:
                        t.wait_ge(vev, 2 * (m - 1) + tc + 1)
                    for kh in range(H):
                        mm = t.matmul(ps_mm[:, tc * 512:(tc + 1) * 512],
                                      wpool[m % 4][:, kh * 128:(kh + 1) * 128],
                                      vsb[:, kh * TT + tc * 512: kh * TT + tc * 512 + 512],
                                      start=(kh == 0), stop=(kh == H - 1))
                        if kh == H - 1:
                            mm.then_inc(tmm, 1)
            t.wait_ge(vev, 32)
            t.wait_ge(ssq, 16)
            for m in range(KD):
                for j in range(4):
                    src = res if j < 2 else vsb
                    tc = j % 2
                    sl = slice(m * TT + tc * 512, m * TT + tc * 512 + 512)
                    t.matmul(ps_st[32 * j:32 * j + 1, 0:512],
                             ones_bf[0:128, 0:1], src[:, sl],
                             start=(m == 0), stop=(m == KD - 1),
                             tile_position=(0, 32 * j)).then_inc(tst, 1)
            t.wait_ge(srw, 2)
            for tc in range(2):
                bs = slice(tc * 512, tc * 512 + 512)
                t.matmul(ps_b1[:, bs], ones_f[0:1, 0:128],
                         rrow[0:1, tc * 512:(tc + 1) * 512], start=True, stop=True).then_inc(tbc, 1)
                t.matmul(ps_b2[:, bs], ones_f[0:1, 0:128],
                         nrow[0:1, tc * 512:(tc + 1) * 512], start=True, stop=True).then_inc(tbc, 1)
            for o in range(KD):
                for tc in range(2):
                    t.wait_ge(vev, 32 + (2 * (o - 1) + tc + 1 if o >= 1 else 0))
                    for ki in range(KD):
                        t.wait_ge(vq, ki + 1)
                        mm = t.matmul(ps_mm[:, tc * 512:(tc + 1) * 512],
                                      w8s[:, (ki * KD + o) * 128:(ki * KD + o) * 128 + 128],
                                      vsb[:, ki * TT + tc * 512: ki * TT + tc * 512 + 512],
                                      start=(ki == 0), stop=(ki == KD - 1))
                        if ki == KD - 1:
                            mm.then_inc(tmm, 1)
            t.wait_ge(vev, 64)
            t.wait_ge(ssq, 32)
            for m in range(KD):
                for j in range(4):
                    src = res if j < 2 else vsb
                    tc = j % 2
                    sl = slice(m * TT + tc * 512, m * TT + tc * 512 + 512)
                    t.matmul(ps_st[32 * j:32 * j + 1, 0:512],
                             ones_bf[0:128, 0:1], src[:, sl],
                             start=(m == 0), stop=(m == KD - 1),
                             tile_position=(0, 32 * j)).then_inc(tst, 1)
            t.wait_ge(srw, 4)
            t.wait_ge(vn, 32)
            for tc in range(2):
                bs = slice(tc * 512, tc * 512 + 512)
                t.matmul(ps_b1[:, bs], ones_f[0:1, 0:128],
                         rrow[0:1, tc * 512:(tc + 1) * 512], start=True, stop=True).then_inc(tbc, 1)
                t.matmul(ps_b2[:, bs], ones_f[0:1, 0:128],
                         nrow[0:1, tc * 512:(tc + 1) * 512], start=True, stop=True).then_inc(tbc, 1)

        @block.scalar
        def _(s):
            s.wait_ge(vev, 32)
            for m in range(KD):
                sl = slice(m * TT, (m + 1) * TT)
                s.activation(vsb[:, sl], res[:, sl], AF.Square,
                             bias=0.0, scale=1.0).then_inc(ssq, 1)
            s.wait_ge(srw, 1)
            s.activation(sdrow[:], sdrow[:], AF.Sqrt, bias=0.0, scale=1.0).then_inc(ssd, 1)
            s.wait_ge(vev, 64)
            for m in range(KD):
                sl = slice(m * TT, (m + 1) * TT)
                s.activation(vsb[:, sl], res[:, sl], AF.Square,
                             bias=0.0, scale=1.0).then_inc(ssq, 1)
            s.wait_ge(srw, 3)
            s.activation(sdrow[:], sdrow[:], AF.Sqrt, bias=0.0, scale=1.0).then_inc(ssd, 1)

        @block.sync
        def _(sy):
            sy.wait_ge(vst, 1)
            sy.dma_start(out=row_sb[0:1, :], in_=stats_sb[0:128:32, 0:512]).then_inc(stg, 16)
            sy.wait_ge(stg, 16)
            sy.wait_ge(vst, 2)
            sy.dma_start(out=row_sb[0:1, :], in_=stats_sb[0:128:32, 0:512]).then_inc(stg, 16)
            for m in range(KD):
                sy.wait_ge(sto, 16 * m)
                sy.wait_ge(vn, 32 + m + 1)
                sy.dma_start(out=out_o[m], in_=outst[m % 2][:]).then_inc(sto, 16)
            sy.wait_ge(sto, 16 * KD)
    return nc




NH = 16
CT = 512


def build_fused():
    nc = bass.Bass("TRN2", target_bir_lowering=False)
    xT = nc.dram_tensor("xT", [128, KD, S], F32, kind="ExternalInput").ap()
    wqk = nc.dram_tensor("wqk", [2 * NH, 128, KD * 128], F32, kind="ExternalInput").ap()
    wv = nc.dram_tensor("wv", [NH, 128, KD * 128], F32, kind="ExternalInput").ap()
    wo = nc.dram_tensor("wo", [KD, 128, KD * 128], F32, kind="ExternalInput").ap()
    w8o = nc.dram_tensor("w8o", [KD, 128, KD * 128], FP8, kind="ExternalInput").ap()
    s_in = nc.dram_tensor("s_in", [1, 1], BF16, kind="ExternalInput").ap()
    w_in = nc.dram_tensor("w_in", [1, 1], F32, kind="ExternalInput").ap()
    out_o = nc.dram_tensor("outT", [KD, 128, TT], F32, kind="ExternalOutput").ap()

    with ExitStack() as ctx:
        ent = ctx.enter_context
        ones_bf = ent(nc.sbuf_tensor("ones_bf", [128, 128], BF16))
        ones_f = ent(nc.sbuf_tensor("ones_f", [1, 128], F32))
        xs = ent(nc.sbuf_tensor("xs", [128, KD * S], BF16))
        qkp = [ent(nc.sbuf_tensor(f"qkp{i}", [128, S], BF16)) for i in range(4)]
        wpool = [ent(nc.sbuf_tensor(f"wp{i}", [128, KD * 128], BF16)) for i in range(4)]
        w8p = [ent(nc.sbuf_tensor(f"w8p{i}", [128, KD * 128], FP8)) for i in range(4)]
        Ebs = [ent(nc.sbuf_tensor(f"Eb{i}", [128, S], BF16)) for i in range(2)]
        Zt = ent(nc.sbuf_tensor("Zt", [128, 4], F32))
        ztmp = ent(nc.sbuf_tensor("ztmp", [128, 2], F32))
        rb = ent(nc.sbuf_tensor("rb", [128, 2], BF16))
        cs_sb = ent(nc.sbuf_tensor("cs_sb", [128, 512], BF16))
        cs16 = ent(nc.sbuf_tensor("cs16", [NH, 2 * CT], BF16))
        csrow = ent(nc.sbuf_tensor("csrow", [1, NH * CT], BF16))
        vv = [ent(nc.sbuf_tensor(f"vv{m}", [128, CT], BF16)) for m in range(KD)]
        res = ent(nc.sbuf_tensor("res", [128, KD * CT], BF16))
        t1 = [ent(nc.sbuf_tensor(f"t1_{i}", [128, CT], F32)) for i in range(2)]
        d8t = [ent(nc.sbuf_tensor(f"d8t{i}", [128, CT], FP8)) for i in range(2)]
        h8t = [ent(nc.sbuf_tensor(f"h8t{i}", [128, CT], FP8)) for i in range(2)]
        hbt = [ent(nc.sbuf_tensor(f"hbt{i}", [128, CT], BF16)) for i in range(2)]
        dbt = [ent(nc.sbuf_tensor(f"dbt{i}", [128, CT], BF16)) for i in range(2)]
        msk = [ent(nc.sbuf_tensor(f"msk{i}", [128, CT], U8)) for i in range(2)]
        mskf = [ent(nc.sbuf_tensor(f"mskf{i}", [128, CT], F32)) for i in range(2)]
        tmpn = [ent(nc.sbuf_tensor(f"tmpn{i}", [128, CT], F32)) for i in range(2)]
        outst = [ent(nc.sbuf_tensor(f"outst{i}", [128, CT], F32)) for i in range(4)]
        stats_sb = ent(nc.sbuf_tensor("stats_sb", [128, 512], F32))
        row_sb = ent(nc.sbuf_tensor("row_sb", [1, 2 * CT], F32))
        murow = ent(nc.sbuf_tensor("murow", [1, CT], F32))
        e2row = ent(nc.sbuf_tensor("e2row", [1, CT], F32))
        sdrow = ent(nc.sbuf_tensor("sdrow", [1, CT], F32))
        rrow = ent(nc.sbuf_tensor("rrow", [1, CT], F32))
        nrow = ent(nc.sbuf_tensor("nrow", [1, CT], F32))
        s_f = ent(nc.sbuf_tensor("s_f", [1, 1], F32))
        srow = ent(nc.sbuf_tensor("srow", [1, 1], F32))
        swrow = ent(nc.sbuf_tensor("swrow", [1, 1], F32))
        rs_sb = ent(nc.sbuf_tensor("rs_sb", [128, 1], F32))
        sw_sb = ent(nc.sbuf_tensor("sw_sb", [128, 1], F32))

        ps_acc = ent(nc.psum_tensor("ps_acc", [128, 1024], F32))
        ps_sc = ent(nc.psum_tensor("ps_sc", [128, 2048], F32))
        ps_cs = ent(nc.psum_tensor("ps_cs", [128, 512], F32))
        ps_bb = ent(nc.psum_tensor("ps_bb", [128, 2], F32))

        ldx = ent(nc.semaphore("ldx"))
        ldx2 = ent(nc.semaphore("ldx2"))
        lds = ent(nc.semaphore("lds"))
        lw = [ent(nc.semaphore(f"lw{i}")) for i in range(4)]
        lw8 = [ent(nc.semaphore(f"lw8{i}")) for i in range(4)]
        oset = ent(nc.semaphore("oset"))
        vpre = ent(nc.semaphore("vpre"))
        tsc = ent(nc.semaphore("tsc"))
        tp = ent(nc.semaphore("tp"))
        cve = ent(nc.semaphore("cve"))
        ts = ent(nc.semaphore("ts"))
        se = ent(nc.semaphore("se"))
        sr = ent(nc.semaphore("sr"))
        tcl = ent(nc.semaphore("tcl"))
        vcs = ent(nc.semaphore("vcs"))
        stgh = ent(nc.semaphore("stgh"))
        stgc = ent(nc.semaphore("stgc"))
        tbc = ent(nc.semaphore("tbc"))
        vmu = ent(nc.semaphore("vmu"))
        tst = ent(nc.semaphore("tst"))
        vst = ent(nc.semaphore("vst"))
        stgr = ent(nc.semaphore("stgr"))
        srw = ent(nc.semaphore("srw"))
        ssd = ent(nc.semaphore("ssd"))
        ssq = ent(nc.semaphore("ssq"))
        vn = ent(nc.semaphore("vn"))
        vq = ent(nc.semaphore("vq"))
        vqd = ent(nc.semaphore("vqd"))
        sq8 = ent(nc.semaphore("sq8"))
        stp = [ent(nc.semaphore("sto0")), ent(nc.semaphore("sto1"))]
        vch = ent(nc.semaphore("vch"))
        block = ent(nc.Block())

        qk_groups = [("qk", s, tc) for s in range(2 * NH) for tc in range(4)]
        sched = {}
        sched[-1] = [(g, hf) for g in range(8) for hf in (0, 1)]
        gidx = 8
        for h in range(NH):
            for qt in range(16):
                sched[(h, qt)] = []
            if h < NH - 1:
                for qt in range(16):
                    sched[(h, qt)] = [(gidx + qt // 2, qt % 2)]
                gidx += 8
        assert gidx == 128
        for qt in range(16):
            sched[(15, qt)] = [(128 + 2 * qt, 0), (128 + 2 * qt, 1)] if False else \
                              [(128 + qt, 0), (128 + qt, 1)]

        GTAB = [("v", 0), ("wo", 0), ("v", 1), ("wo", 1), ("ffn", 0), ("ffn", 1)]

        def group_info(g):
            pg = g - 128
            kind, ch = GTAB[pg // 16]
            return kind, ch, pg % 16

        def slab_of_group(g):
            if g < 128:
                return g // 4
            kind, ch, idx = group_info(g)
            if kind == "v":
                return 32 + ch * 32 + idx
            if kind == "wo":
                return 48 + ch * 32 + idx
            return None
        n_groups_total = 128 + 6 * 16
        group_end_of_slab = {}
        for g in range(n_groups_total):
            sl_ = slab_of_group(g)
            if sl_ is not None:
                group_end_of_slab[sl_] = g + 1

        @block.gpsimd
        def _(g_):
            g_.memset(ones_bf[:], 1.0).then_inc(oset, 1)
            g_.memset(ones_f[:], 1.0).then_inc(oset, 1)
            g_.dma_start(out=s_f[:], in_=s_in).then_inc(lds, 16)
            g_.dma_start(out=swrow[:], in_=w_in).then_inc(lds, 16)
            g_.dma_start(out=xs[:, 0:8 * S], in_=xT[:, 0:8, :]).then_inc(ldx, 16)
            g_.dma_start(out=xs[:, 8 * S:KD * S], in_=xT[:, 8:KD, :]).then_inc(ldx2, 16)
            stream = [("w", s, wqk[s]) for s in range(2 * NH)]
            for ch in range(2):
                stream += [("w", 32 + ch * 32 + i, wv[i]) for i in range(NH)]
                stream += [("w", 48 + ch * 32 + i, wo[i]) for i in range(KD)]
            stream += [("8", j, w8o[j % KD]) for j in range(2 * KD)]
            for kind, s, src_ap in stream:
                if kind == "w":
                    if s >= 4:
                        g_.wait_ge(lw[s % 4], 16 * (s // 4))
                        g_.wait_ge(tp, group_end_of_slab[s - 4])
                    g_.dma_start(out=wpool[s % 4][:], in_=src_ap).then_inc(lw[s % 4], 16)
                else:
                    if s >= 4:
                        g_.wait_ge(lw8[s % 4], 16 * (s // 4))
                        chp, op = divmod(s - 4, KD)
                        g_.wait_ge(tp, 192 + chp * 16 + op + 1)
                    g_.dma_start(out=w8p[s % 4][:], in_=src_ap).then_inc(lw8[s % 4], 16)

        vvB = [qkp[r // 4][:, (r % 4) * 512:(r % 4) * 512 + 512] for r in range(KD)]
        resB = [xs[:, m * S + S - 512: m * S + S] for m in range(KD)]
        sqB = [xs[:, m * S + 2 * CT: m * S + 3 * CT] for m in range(KD)]

        @block.tensor
        def _(t):
            t.wait_ge(oset, 2)
            t.wait_ge(vpre, 2)
            t.matmul(ps_bb[:, 0:1], ones_f[0:1, 0:128], srow[0:1, 0:1],
                     start=True, stop=True).then_inc(tsc, 1)
            t.matmul(ps_bb[:, 1:2], ones_f[0:1, 0:128], swrow[0:1, 0:1],
                     start=True, stop=True).then_inc(tsc, 1)

            def emit_group_half(g, hf):
                bank = g % 2
                osl = slice(bank * 512, bank * 512 + 512)
                kds = range(8 * hf, 8 * hf + 8)
                if hf == 0:
                    if g >= 2:
                        t.wait_ge(cve, g - 1)
                if g < 128:
                    s, tc = g // 4, g % 4
                    if hf == 0:
                        t.wait_ge(lw[s % 4], 16 * (s // 4 + 1))
                    if g < 8:
                        t.wait_ge(ldx if hf == 0 else ldx2, 16)
                    for kd in kds:
                        mm = t.matmul(ps_acc[:, osl], wpool[s % 4][:, kd * 128:(kd + 1) * 128],
                                      xs[:, kd * S + tc * 512: kd * S + tc * 512 + 512],
                                      start=(kd == 0), stop=(kd == KD - 1),
                                      skip_group_check=True)
                        if kd == KD - 1:
                            mm.then_inc(tp, 1)
                    return
                kind, ch, idx = group_info(g)
                if kind == "v":
                    s = 32 + ch * 32 + idx
                    if hf == 0:
                        t.wait_ge(lw[s % 4], 16 * (s // 4 + 1))
                    for kd in kds:
                        mm = t.matmul(ps_acc[:, osl], wpool[s % 4][:, kd * 128:(kd + 1) * 128],
                                      xs[:, kd * S + ch * CT: kd * S + ch * CT + CT],
                                      start=(kd == 0), stop=(kd == KD - 1),
                                      skip_group_check=True)
                        if kd == KD - 1:
                            mm.then_inc(tp, 1)
                    return
                if kind == "wo":
                    s = 48 + ch * 32 + idx
                    vsrc = vv if ch == 0 else vvB
                    if hf == 0:
                        t.wait_ge(lw[s % 4], 16 * (s // 4 + 1))
                        t.wait_ge(vmu, (ch + 1) * NH)
                    for kh in kds:
                        mm = t.matmul(ps_acc[:, osl], wpool[s % 4][:, kh * 128:(kh + 1) * 128],
                                      vsrc[kh][:, 0:CT] if ch == 0 else vsrc[kh],
                                      start=(kh == 0), stop=(kh == NH - 1),
                                      skip_group_check=True)
                        if kh == NH - 1:
                            mm.then_inc(tp, 1)
                    return
                j = ch * KD + idx
                if hf == 0:
                    t.wait_ge(lw8[j % 4], 16 * (j // 4 + 1))
                    t.wait_ge(vq, (ch + 1) * KD)
                for ki in kds:
                    mm = t.matmul(ps_acc[:, osl], w8p[j % 4][:, ki * 128:(ki + 1) * 128],
                                  vv[ki][:, 0:CT],
                                  start=(ki == 0), stop=(ki == KD - 1),
                                  skip_group_check=True)
                    if ki == KD - 1:
                        mm.then_inc(tp, 1)

            def emit_group(g):
                emit_group_half(g, 0)
                emit_group_half(g, 1)

            ng = 0
            for g, hf in sched[-1]:
                emit_group_half(g, hf)
            def emit_colsum(h, m):
                n = h * 16 + m
                t.wait_ge(sr, n + 1)
                if m == 0:
                    t.wait_ge(vcs, h)
                for j in range(4):
                    t.matmul(ps_cs[32 * j:32 * j + 1, 0:256],
                             rb[:, m % 2:m % 2 + 1], Ebs[m % 2][:, j * 256:(j + 1) * 256],
                             start=(m == 0), stop=(m == 15),
                             tile_position=(0, 32 * j)).then_inc(tcl, 1)

            for h in range(NH):
                for qt in range(16):
                    n = h * 16 + qt
                    if qt == 0:
                        t.wait_ge(cve, 8 * (h + 1))
                    t.wait_ge(se, max(0, 2 * n - 1))
                    for kc in range(2):
                        t.matmul(ps_sc[:, kc * 512:(kc + 1) * 512],
                                 qkp[(2 * h) % 4][:, qt * 128:(qt + 1) * 128],
                                 qkp[(2 * h + 1) % 4][:, kc * 512:(kc + 1) * 512],
                                 start=True, stop=True).then_inc(ts, 1)
                    for g, hf in sched[(h, qt)]:
                        emit_group_half(g, hf)
                    t.wait_ge(se, 2 * n)
                    for kc in range(2, 4):
                        t.matmul(ps_sc[:, kc * 512:(kc + 1) * 512],
                                 qkp[(2 * h) % 4][:, qt * 128:(qt + 1) * 128],
                                 qkp[(2 * h + 1) % 4][:, kc * 512:(kc + 1) * 512],
                                 start=True, stop=True).then_inc(ts, 1)
                    if qt >= 1:
                        emit_colsum(h, qt - 1)
                emit_colsum(h, 15)
            t.wait_ge(stgc, 16)
            t.wait_ge(se, 512)
            for hh in range(NH):
                bs = slice((hh % 2) * 512, (hh % 2) * 512 + 512)
                if hh >= 2:
                    t.wait_ge(vmu, hh - 1)
                t.matmul(ps_sc[:, bs], ones_bf[0:1, 0:128],
                         csrow[0:1, hh * CT:(hh + 1) * CT],
                         start=True, stop=True).then_inc(tbc, 1)
            for i in range(16):
                emit_group(144 + i)
            for m in range(KD):
                t.wait_ge(cve, 144 + m + 1)
                t.wait_ge(ssq, m + 1)
                t.matmul(ps_cs[0:1, 0:512], ones_bf[0:128, 0:1],
                         res[:, m * CT:(m + 1) * CT],
                         start=(m == 0), stop=(m == KD - 1),
                         tile_position=(0, 0)).then_inc(tst, 1)
                t.matmul(ps_cs[32:33, 0:512], ones_bf[0:128, 0:1],
                         vv[m][:, 0:CT],
                         start=(m == 0), stop=(m == KD - 1),
                         tile_position=(0, 32)).then_inc(tst, 1)
            t.wait_ge(srw, 2)
            t.matmul(ps_sc[:, 512:1024], ones_f[0:1, 0:128], rrow[0:1, 0:CT],
                     start=True, stop=True).then_inc(tbc, 1)
            t.matmul(ps_sc[:, 1024:1536], ones_f[0:1, 0:128], nrow[0:1, 0:CT],
                     start=True, stop=True).then_inc(tbc, 1)
            for i in range(16):
                emit_group(160 + i)
            t.wait_ge(stgc, 32)
            for hh in range(NH):
                bs = slice(0, 512) if hh % 2 == 0 else slice(1536, 2048)
                t.wait_ge(vmu, NH + hh - 1)
                t.matmul(ps_sc[:, bs], ones_bf[0:1, 0:128],
                         csrow[0:1, hh * CT:(hh + 1) * CT],
                         start=True, stop=True).then_inc(tbc, 1)
            for i in range(16):
                emit_group(176 + i)
            for i in range(16):
                emit_group(192 + i)
            t.wait_ge(vst, 1)
            for m in range(KD):
                t.wait_ge(cve, 192 + m + 1)
                t.wait_ge(ssq, 2 * KD + m + 1)
                t.matmul(ps_cs[0:1, 0:512], ones_bf[0:128, 0:1],
                         res[:, m * CT:(m + 1) * CT],
                         start=(m == 0), stop=(m == KD - 1),
                         tile_position=(0, 0)).then_inc(tst, 1)
                t.matmul(ps_cs[32:33, 0:512], ones_bf[0:128, 0:1],
                         vv[m][:, 0:CT],
                         start=(m == 0), stop=(m == KD - 1),
                         tile_position=(0, 32)).then_inc(tst, 1)
            t.wait_ge(srw, 4)
            t.wait_ge(vn, KD)
            t.matmul(ps_sc[:, 512:1024], ones_f[0:1, 0:128], rrow[0:1, 0:CT],
                     start=True, stop=True).then_inc(tbc, 1)
            t.matmul(ps_sc[:, 1024:1536], ones_f[0:1, 0:128], nrow[0:1, 0:CT],
                     start=True, stop=True).then_inc(tbc, 1)
            t.wait_ge(vst, 2)
            for m in range(KD):
                t.wait_ge(cve, 176 + m + 1)
                t.wait_ge(ssq, KD + m + 1)
                t.matmul(ps_cs[0:1, 0:512], ones_bf[0:128, 0:1],
                         resB[m], start=(m == 0), stop=(m == KD - 1),
                         tile_position=(0, 0)).then_inc(tst, 1)
                t.matmul(ps_cs[32:33, 0:512], ones_bf[0:128, 0:1],
                         sqB[m], start=(m == 0), stop=(m == KD - 1),
                         tile_position=(0, 32)).then_inc(tst, 1)
            t.wait_ge(srw, 6)
            t.wait_ge(vmu, 2 * NH)
            t.matmul(ps_sc[:, 0:512], ones_f[0:1, 0:128], rrow[0:1, 0:CT],
                     start=True, stop=True).then_inc(tbc, 1)
            t.matmul(ps_sc[:, 1536:2048], ones_f[0:1, 0:128], nrow[0:1, 0:CT],
                     start=True, stop=True).then_inc(tbc, 1)
            for i in range(16):
                emit_group(208 + i)
            t.wait_ge(vst, 3)
            for m in range(KD):
                t.wait_ge(cve, 208 + m + 1)
                t.wait_ge(ssq, 3 * KD + m + 1)
                t.matmul(ps_cs[0:1, 0:512], ones_bf[0:128, 0:1],
                         res[:, m * CT:(m + 1) * CT],
                         start=(m == 0), stop=(m == KD - 1),
                         tile_position=(0, 0)).then_inc(tst, 1)
                t.matmul(ps_cs[32:33, 0:512], ones_bf[0:128, 0:1],
                         vv[m][:, 0:CT],
                         start=(m == 0), stop=(m == KD - 1),
                         tile_position=(0, 32)).then_inc(tst, 1)
            t.wait_ge(srw, 8)
            t.wait_ge(vn, 3 * KD)
            t.matmul(ps_sc[:, 512:1024], ones_f[0:1, 0:128], rrow[0:1, 0:CT],
                     start=True, stop=True).then_inc(tbc, 1)
            t.matmul(ps_sc[:, 1024:1536], ones_f[0:1, 0:128], nrow[0:1, 0:CT],
                     start=True, stop=True).then_inc(tbc, 1)

        @block.scalar
        def _(s_):
            for h in range(NH):
                for qt in range(16):
                    n = h * 16 + qt
                    s_.wait_ge(ts, 4 * n + 2)
                    if n >= 1:
                        s_.wait_ge(tcl, 4 * (n - 1))
                    p = n % 2
                    s_.activation(Ebs[p][:, 0:1024], ps_sc[:, 0:1024], AF.Exp,
                                  bias=0.0, scale=ISQ,
                                  accum_out=Zt[:, 2 * p:2 * p + 1]).then_inc(se, 1)
                    s_.wait_ge(ts, 4 * n + 4)
                    s_.activation(Ebs[p][:, 1024:2048], ps_sc[:, 1024:2048], AF.Exp,
                                  bias=0.0, scale=ISQ,
                                  accum_out=Zt[:, 2 * p + 1:2 * p + 2]).then_inc(se, 1)
            def qdq_helpers(ch):
                for p in range(KD // 2):
                    pp = ch * KD // 2 + p
                    if pp >= 1:
                        s_.wait_ge(vq, 2 * pp)
                    for j in range(2):
                        s_.wait_ge(vqd, 4 * pp + 2 * j + 2)
                        s_.activation(d8t[j][:], t1[j][:], AF.Copy,
                                      bias=0.0, scale=1.0).then_inc(sq8, 1)
                        s_.wait_ge(sq8, 8 * pp + 4 * j + 1)
                        s_.activation(dbt[j][:], d8t[j][:], AF.Copy,
                                      bias=0.0, scale=1.0).then_inc(sq8, 1)
                        s_.activation(hbt[j][:], h8t[j][:], AF.Copy,
                                      bias=0.0, scale=2.0).then_inc(sq8, 1)
                        s_.activation(mskf[j][:], h8t[j][:], AF.Square,
                                      bias=0.0, scale=2.0).then_inc(sq8, 1)

            def squares(cbase, srcs, dsts, tpw=0, tstw=0):
                if tpw:
                    s_.wait_ge(tp, tpw)
                if tstw:
                    s_.wait_ge(tst, tstw)
                for m in range(KD):
                    s_.wait_ge(cve, cbase + m + 1)
                    s_.activation(dsts(m), srcs(m), AF.Square,
                                  bias=0.0, scale=1.0).then_inc(ssq, 1)

            def sqrt_stage(k):
                s_.wait_ge(srw, 2 * k + 1)
                s_.activation(sdrow[:], sdrow[:], AF.Sqrt,
                              bias=0.0, scale=1.0).then_inc(ssd, 1)

            res_m = lambda m: res[:, m * CT:(m + 1) * CT]
            vv_m = lambda m: vv[m][:, 0:CT]
            squares(144, res_m, vv_m, tpw=160)
            sqrt_stage(0)
            qdq_helpers(0)
            squares(176, lambda m: resB[m], lambda m: sqB[m])
            squares(192, res_m, vv_m, tpw=208, tstw=32)
            sqrt_stage(1)
            sqrt_stage(2)
            qdq_helpers(1)
            squares(208, res_m, vv_m, tpw=224, tstw=96)
            sqrt_stage(3)

        @block.vector
        def _(v):
            vc = [0]
            def VI(instr):
                vc[0] += 1
                instr.then_inc(vch, 1)
            def VBAR():
                v.wait_ge(vch, vc[0])
            v.wait_ge(lds, 32)
            v.reciprocal(srow[:], s_f[:]).then_inc(vpre, 1)
            v.tensor_tensor(swrow[:], swrow[:], s_f[:], ALU.mult).then_inc(vpre, 1)
            v.wait_ge(tsc, 2)
            v.tensor_copy(rs_sb[:], ps_bb[:, 0:1]).then_inc(vpre, 1)
            v.tensor_copy(sw_sb[:], ps_bb[:, 1:2]).then_inc(vpre, 1)

            def consume_group(g):
                bank = g % 2
                osl = slice(bank * 512, bank * 512 + 512)
                v.wait_ge(tp, g + 1)
                if g < 128:
                    s, tc = g // 4, g % 4
                    if s >= 4:
                        v.wait_ge(ts, 64 * ((s - 4) // 2 + 1))
                    v.tensor_copy(qkp[s % 4][:, tc * 512:(tc + 1) * 512],
                                  ps_acc[:, osl]).then_inc(cve, 1)
                    return
                kind, ch, idx = group_info(g)
                if kind == "v":
                    dst = vv[idx][:, 0:CT] if ch == 0 else vvB[idx]
                    v.tensor_copy(dst, ps_acc[:, osl]).then_inc(cve, 1)
                    return
                if kind == "wo":
                    m = idx
                    sl = slice(m * S + ch * CT, m * S + ch * CT + CT)
                    dst = res[:, m * CT:(m + 1) * CT] if ch == 0 else resB[m]
                    v.tensor_tensor(dst, ps_acc[:, osl],
                                    xs[:, sl], ALU.add).then_inc(cve, 1)
                    return
                o = idx
                xnsl = slice(o * S + ch * CT, o * S + ch * CT + CT)
                v.scalar_tensor_tensor(res[:, o * CT:(o + 1) * CT], ps_acc[:, osl],
                                       sw_sb[:], xs[:, xnsl],
                                       ALU.mult, ALU.add).then_inc(cve, 1)

            for g, hf in sched[-1]:
                if hf == 1:
                    consume_group(g)
            for h in range(NH):
                for qt in range(16):
                    n = h * 16 + qt
                    for g, hf in sched[(h, qt)]:
                        if hf == 1:
                            consume_group(g)
                    v.wait_ge(se, 2 * (n + 1))
                    p = n % 2
                    VI(v.tensor_tensor(ztmp[:, p:p + 1], Zt[:, 2 * p:2 * p + 1],
                                       Zt[:, 2 * p + 1:2 * p + 2], ALU.add))
                    VBAR()
                    with nc.allow_low_precision("1/Z stationary is bf16"):
                        v.reciprocal(rb[:, p:p + 1],
                                     ztmp[:, p:p + 1]).then_inc(sr, 1)
                v.wait_ge(tcl, 64 * (h + 1))
                v.wait_ge(stgh, 16 * h)
                for j in range(4):
                    cp = v.tensor_copy(cs_sb[32 * j:32 * j + 1, 0:256],
                                       ps_cs[32 * j:32 * j + 1, 0:256])
                if True:
                    cp.then_inc(vcs, 1)
            def rows_stage(k, tbc_guard=0):
                v.wait_ge(tst, 32 * (k + 1))
                v.tensor_copy(stats_sb[0:1, 0:512], ps_cs[0:1, 0:512])
                v.tensor_copy(stats_sb[32:33, 0:512], ps_cs[32:33, 0:512]).then_inc(vst, 1)
                if tbc_guard:
                    v.wait_ge(tbc, tbc_guard)
                v.wait_ge(stgr, 16 * (k + 1))
                VI(v.tensor_scalar(murow[:], row_sb[0:1, 0:CT], 1.0 / 2048.0, None, ALU.mult))
                VI(v.tensor_scalar(e2row[:], row_sb[0:1, CT:2 * CT], 1.0 / 2048.0, None, ALU.mult))
                VBAR()
                VI(v.tensor_tensor(sdrow[:], murow[:], murow[:], ALU.mult))
                VBAR()
                VI(v.tensor_tensor(sdrow[:], e2row[:], sdrow[:], ALU.subtract))
                VBAR()
                v.tensor_scalar(sdrow[:], sdrow[:], 1e-5, None, ALU.add).then_inc(srw, 1)
                v.wait_ge(ssd, k + 1)
                VI(v.reciprocal(rrow[:], sdrow[:]))
                VBAR()
                VI(v.tensor_tensor(nrow[:], murow[:], rrow[:], ALU.mult))
                VBAR()
                v.tensor_scalar(nrow[:], nrow[:], -1.0, None, ALU.mult).then_inc(srw, 1)

            def normalize(srcs, dsts, tbc_need, vn_base, rsl=slice(512, 1024),
                          nsl=slice(1024, 1536)):
                v.wait_ge(tbc, tbc_need)
                for m in range(KD):
                    tb = tmpn[m % 2]
                    if m >= 2:
                        v.wait_ge(vn, vn_base + m - 1)
                    VI(v.tensor_tensor(tb[:], srcs(m), ps_sc[:, rsl], ALU.mult))
                    VBAR()
                    v.tensor_tensor(dsts(m), tb[:], ps_sc[:, nsl],
                                    ALU.add).then_inc(vn, 1)

            def qdq(ch, vn_need):
                v.wait_ge(vpre, 4)
                v.wait_ge(vn, vn_need)
                for p in range(KD // 2):
                    pp = ch * KD // 2 + p
                    if pp >= 1:
                        v.wait_ge(sq8, 8 * pp)
                    for j in range(2):
                        m = 2 * p + j
                        sl = slice(m * S + ch * CT, m * S + ch * CT + CT)
                        v.tensor_scalar(t1[j][:], xs[:, sl], rs_sb[:], -448.0,
                                        ALU.mult, ALU.max).then_inc(vqd, 1)
                        v.wait_ge(vqd, 4 * pp + 2 * j + 1)
                        v.tensor_scalar(h8t[j][:], t1[j][:], 448.0, 0.5,
                                        ALU.min, ALU.mult).then_inc(vqd, 1)
                    v.wait_ge(sq8, 8 * pp + 8)
                    for j in range(2):
                        m = 2 * p + j
                        VI(v.tensor_scalar(msk[j][:], mskf[j][:], 57600.0, None, ALU.is_ge))
                        VBAR()
                        v.select(vv[m][:, 0:CT], msk[j][:], hbt[j][:], dbt[j][:],
                                 add_drain=True).then_inc(vq, 1)

            for hh in range(NH):
                bs = slice((hh % 2) * 512, (hh % 2) * 512 + 512)
                v.wait_ge(tbc, hh + 1)
                v.tensor_tensor(vv[hh][:, 0:CT], vv[hh][:, 0:CT],
                                ps_sc[:, bs], ALU.mult).then_inc(vmu, 1)
            for i in range(16):
                consume_group(144 + i)
            rows_stage(0)
            normalize(lambda m: res[:, m * CT:(m + 1) * CT],
                      lambda m: xs[:, m * S: m * S + CT], 18, 0)
            for i in range(16):
                consume_group(160 + i)
            for hh in range(NH):
                bs = slice(0, 512) if hh % 2 == 0 else slice(1536, 2048)
                v.wait_ge(tbc, 19 + hh)
                v.tensor_tensor(vvB[hh], vvB[hh], ps_sc[:, bs], ALU.mult).then_inc(vmu, 1)
            qdq(0, KD)
            for i in range(16):
                consume_group(176 + i)
            for i in range(16):
                consume_group(192 + i)
            rows_stage(1)
            rows_stage(2, tbc_guard=36)
            normalize(lambda m: resB[m],
                      lambda m: xs[:, m * S + CT: m * S + 2 * CT], 38, KD,
                      rsl=slice(0, 512), nsl=slice(1536, 2048))
            qdq(1, 2 * KD)
            v.wait_ge(tbc, 36)
            for m in range(KD):
                ob = outst[m % 4]
                tb = tmpn[m % 2]
                if m >= 4:
                    v.wait_ge(stp[m % 2], 16 * ((m - 4) // 2 + 1))
                if m >= 2:
                    v.wait_ge(vn, 2 * KD + m - 1)
                VI(v.tensor_tensor(tb[:], res[:, m * CT:(m + 1) * CT],
                                   ps_sc[:, 512:1024], ALU.mult))
                VBAR()
                v.tensor_tensor(ob[:, 0:CT], tb[:], ps_sc[:, 1024:1536],
                                ALU.add).then_inc(vn, 1)
            for i in range(16):
                consume_group(208 + i)
            rows_stage(3)
            v.wait_ge(tbc, 40)
            for m in range(KD):
                ob = outst[m % 4]
                tb = tmpn[m % 2]
                v.wait_ge(stp[m % 2], 16 * ((KD + m - 4) // 2 + 1))
                if m >= 2:
                    v.wait_ge(vn, 3 * KD + m - 1)
                VI(v.tensor_tensor(tb[:], res[:, m * CT:(m + 1) * CT],
                                   ps_sc[:, 512:1024], ALU.mult))
                VBAR()
                v.tensor_tensor(ob[:, 0:CT], tb[:], ps_sc[:, 1024:1536],
                                ALU.add).then_inc(vn, 1)

        @block.sync
        def _(sy):
            for h in range(NH):
                sy.wait_ge(stgh, 16 * h)
                sy.wait_ge(vcs, h + 1)
                sy.dma_start(out=cs16[h:h + 1, 0:2 * CT],
                             in_=cs_sb[0:128:32, 0:256]).then_inc(stgh, 16)
            sy.wait_ge(stgh, 16 * NH)
            sy.dma_start(out=csrow[0:1, :],
                         in_=cs16[0:NH, 0:CT]).then_inc(stgc, 16)
            sy.wait_ge(stgc, 16)
            sy.wait_ge(tbc, 16)
            sy.dma_start(out=csrow[0:1, :],
                         in_=cs16[0:NH, CT:2 * CT]).then_inc(stgc, 16)
            for k in range(3):
                sy.wait_ge(stgr, 16 * k)
                sy.wait_ge(vst, k + 1)
                sy.dma_start(out=row_sb[0:1, :],
                             in_=stats_sb[0:64:32, 0:512]).then_inc(stgr, 16)
            for m in range(KD):
                sy.wait_ge(stp[m % 2], 16 * (m // 2))
                sy.wait_ge(vn, 2 * KD + m + 1)
                sy.dma_start(out=out_o[m][:, 0:CT],
                             in_=outst[m % 4][:]).then_inc(stp[m % 2], 16)
            sy.wait_ge(stgr, 48)
            sy.wait_ge(vst, 4)
            sy.dma_start(out=row_sb[0:1, :],
                         in_=stats_sb[0:64:32, 0:512]).then_inc(stgr, 16)
            for m in range(KD):
                sy.wait_ge(stp[m % 2], 16 * ((KD + m) // 2))
                sy.wait_ge(vn, 3 * KD + m + 1)
                sy.dma_start(out=out_o[m][:, CT:2 * CT],
                             in_=outst[m % 4][:]).then_inc(stp[m % 2], 16)
            sy.wait_ge(stp[0], 16 * KD)
            sy.wait_ge(stp[1], 16 * KD)
            sy.wait_ge(stgc, 32)
    return nc


_cache = {}


def _get(name, builder):
    if name not in _cache:
        _cache[name] = builder()
    return _cache[name]


FUSED = True


def kernel(L0_attn_x, L0_attn_w_qkv, L0_attn_w_o, L0_ffn_weight,
           L0_ffn_weight_scale, L0_ffn_input_scale):
    x = np.asarray(L0_attn_x)
    wqkv = np.asarray(L0_attn_w_qkv)
    w_o = np.asarray(L0_attn_w_o)
    w8 = np.asarray(L0_ffn_weight)
    wsc = np.asarray(L0_ffn_weight_scale, dtype=np.float32).reshape(1, 1)
    isc = np.asarray(L0_ffn_input_scale).reshape(1, 1)

    def tile_slabs(A):
        ncol = A.shape[1]
        return np.ascontiguousarray(
            A.reshape(KD, 128, ncol // 128, 128).transpose(2, 1, 0, 3)
        ).reshape(ncol // 128, 128, KD * 128)

    if FUSED:
        wqk32 = tile_slabs(np.concatenate(
            [c for h in range(H)
             for c in (wqkv[:, h * 128:(h + 1) * 128],
                       wqkv[:, 2048 + h * 128: 2048 + (h + 1) * 128])], axis=1))
        wv16 = tile_slabs(wqkv[:, 4096:6144])
        wo_t = tile_slabs(w_o)
        w8o = tile_slabs(np.ascontiguousarray(w8.T))
        xTr = []
        for c in range(8):
            b, g = c // 2, c % 2
            xb = x[b].T
            rot = np.concatenate([xb[:, g * TT:(g + 1) * TT],
                                  xb[:, (1 - g) * TT:(2 - g) * TT]], axis=1)
            xTr.append(np.ascontiguousarray(
                rot.reshape(KD, 128, S).transpose(1, 0, 2)))
        nc = _get("fused", build_fused)
        ins = [dict(xT=xTr[c], wqk=wqk32, wv=wv16, wo=wo_t, w8o=w8o,
                    s_in=isc, w_in=wsc) for c in range(8)]
        r = run_bass_kernel_spmd(nc, ins, core_ids=list(range(8)))
        out = np.empty((B, S, D), dtype=np.float32)
        for c in range(8):
            b, g = c // 2, c % 2
            oT = np.asarray(r.results[c]["outT"])
            out[b, g * TT:(g + 1) * TT, :] = oT.transpose(2, 0, 1).reshape(TT, D)
        return out

    xT2 = [np.ascontiguousarray(x[b].T.reshape(KD, 128, S).transpose(1, 0, 2))
           for b in range(B)]
    wqk_t = []
    wv_t = []
    for g in range(2):
        qcols = wqkv[:, g * 1024:(g + 1) * 1024]
        kcols = wqkv[:, 2048 + g * 1024: 2048 + (g + 1) * 1024]
        vcols = wqkv[:, 4096 + g * 1024: 4096 + (g + 1) * 1024]
        wqk_t.append(tile_slabs(np.concatenate([qcols, kcols], axis=1)))
        wv_t.append(tile_slabs(vcols))

    nc1 = _get("l1", build_l1)
    in1 = [dict(xT=xT2[c // 2], wqk=wqk_t[c % 2], wv=wv_t[c % 2]) for c in range(8)]
    res1 = run_bass_kernel_spmd(nc1, in1, core_ids=list(range(8)))
    vals = [np.asarray(res1.results[c]["vals"]) for c in range(8)]

    vals16 = []
    for c in range(8):
        b, g = c // 2, c % 2
        lo, hi = g * TT, (g + 1) * TT
        v16 = np.concatenate([vals[2 * b][:, :, lo:hi], vals[2 * b + 1][:, :, lo:hi]],
                             axis=0)
        vals16.append(np.ascontiguousarray(v16.transpose(1, 0, 2)))

    wo_t = tile_slabs(w_o)
    w8T = np.ascontiguousarray(
        w8.T.reshape(KD, 128, KD, 128).transpose(1, 0, 2, 3)
    ).reshape(128, KD * KD * 128)
    xTm = [np.ascontiguousarray(xT2[b][:, :, g * TT:(g + 1) * TT])
           for b in range(B) for g in range(2)]

    nc2 = _get("l2", build_l2)
    in2 = [dict(vals16=vals16[c], xTm=xTm[c], wo=wo_t, w8=w8T,
                s_in=isc, w_in=wsc) for c in range(8)]
    res2 = run_bass_kernel_spmd(nc2, in2, core_ids=list(range(8)))

    out = np.empty((B, S, D), dtype=np.float32)
    for c in range(8):
        b, g = c // 2, c % 2
        oT = np.asarray(res2.results[c]["outT"])
        out[b, g * TT:(g + 1) * TT, :] = oT.transpose(2, 0, 1).reshape(TT, D)
    return out


# version 20
# speedup vs baseline: 1.0642x; 1.0421x over previous
import numpy as np
import ml_dtypes
from contextlib import ExitStack

import concourse.bass as bass
import concourse.mybir as mybir
from concourse.bass_utils import run_bass_kernel_spmd

F32 = mybir.dt.float32
BF16 = mybir.dt.bfloat16
FP8 = mybir.dt.float8e4
U8 = mybir.dt.uint8
AF = mybir.ActivationFunctionType
ALU = mybir.AluOpType

B, S, D, H, DH = 4, 2048, 2048, 16, 128
KD = D // 128
TT = S // 2
ISQ = 1.0 / float(np.sqrt(np.float32(128.0)))
NSLAB1 = 24


def build_l1():
    nc = bass.Bass("TRN2", target_bir_lowering=False)
    xT = nc.dram_tensor("xT", [128, KD, S], F32, kind="ExternalInput").ap()
    wqk = nc.dram_tensor("wqk", [16, 128, KD * 128], F32, kind="ExternalInput").ap()
    wv = nc.dram_tensor("wv", [8, 128, KD * 128], F32, kind="ExternalInput").ap()
    vals_o = nc.dram_tensor("vals", [8, 128, S], BF16, kind="ExternalOutput").ap()

    with ExitStack() as ctx:
        ent = ctx.enter_context
        ones_bf = ent(nc.sbuf_tensor("ones_bf", [128, 128], BF16))
        xs = ent(nc.sbuf_tensor("xs", [128, KD * S], BF16))
        wpool = [ent(nc.sbuf_tensor(f"wp{i}", [128, KD * 128], BF16)) for i in range(4)]
        qk = [ent(nc.sbuf_tensor(f"qk{m}", [128, S], BF16)) for m in range(16)]
        vv = [ent(nc.sbuf_tensor(f"vv{m}", [128, S], BF16)) for m in range(8)]
        Eb = ent(nc.sbuf_tensor("Eb", [128, S], BF16))
        Zt = ent(nc.sbuf_tensor("Zt", [128, 1], F32))
        rf = ent(nc.sbuf_tensor("rf", [128, 1], F32))
        rb = ent(nc.sbuf_tensor("rb", [128, 1], BF16))
        cs_sb = ent(nc.sbuf_tensor("cs_sb", [128, 512], BF16))
        csrow = ent(nc.sbuf_tensor("csrow", [1, S], BF16))

        ps_proj = ent(nc.psum_tensor("ps_proj", [128, 2048], F32))
        ps_cs = ent(nc.psum_tensor("ps_cs", [128, 512], F32))
        ps_bc = ent(nc.psum_tensor("ps_bc", [128, 512], F32))

        ldx = ent(nc.semaphore("ldx"))
        ldx2 = ent(nc.semaphore("ldx2"))
        lw = [ent(nc.semaphore(f"lw{i}")) for i in range(4)]
        tp = ent(nc.semaphore("tp"))
        ve = ent(nc.semaphore("ve"))
        ts = ent(nc.semaphore("ts"))
        se = ent(nc.semaphore("se"))
        sr = ent(nc.semaphore("sr"))
        tcl = ent(nc.semaphore("tcl"))
        vcs = ent(nc.semaphore("vcs"))
        stg = ent(nc.semaphore("stg"))
        tbc = ent(nc.semaphore("tbc"))
        vb = ent(nc.semaphore("vb"))
        stv = ent(nc.semaphore("stv"))
        oset = ent(nc.semaphore("oset"))
        block = ent(nc.Block())

        @block.gpsimd
        def _(g):
            g.memset(ones_bf[:], 1.0).then_inc(oset, 1)
            g.dma_start(out=xs[:], in_=xT).then_inc(ldx, 16)
            for m in range(NSLAB1):
                if m >= 4:
                    g.wait_ge(lw[m % 4], 16 * (m // 4))
                    g.wait_ge(tp, 4 * (m - 3))
                src = wqk[m] if m < 16 else wv[m - 16]
                g.dma_start(out=wpool[m % 4][:], in_=src).then_inc(lw[m % 4], 16)

        @block.tensor
        def _(t):
            t.wait_ge(ldx, 16)
            for m in range(NSLAB1):
                t.wait_ge(lw[m % 4], 16 * (m // 4 + 1))
                for tc in range(4):
                    if m >= 1:
                        t.wait_ge(ve, 4 * (m - 1) + tc + 1)
                    for kd in range(KD):
                        mm = t.matmul(ps_proj[:, tc * 512:(tc + 1) * 512],
                                      wpool[m % 4][:, kd * 128:(kd + 1) * 128],
                                      xs[:, kd * S + tc * 512: kd * S + tc * 512 + 512],
                                      start=(kd == 0), stop=(kd == KD - 1))
                        if kd == KD - 1:
                            mm.then_inc(tp, 1)
            t.wait_ge(ve, 4 * NSLAB1)
            for h in range(8):
                for qt in range(16):
                    n = h * 16 + qt
                    t.wait_ge(se, n)
                    for kc in range(4):
                        t.matmul(ps_proj[:, kc * 512:(kc + 1) * 512],
                                 qk[h][:, qt * 128:(qt + 1) * 128],
                                 qk[8 + h][:, kc * 512:(kc + 1) * 512],
                                 start=True, stop=True).then_inc(ts, 1)
                    t.wait_ge(se, n + 1)
                    t.wait_ge(sr, n + 1)
                    if qt == 0:
                        t.wait_ge(vcs, h)
                    for j in range(4):
                        t.matmul(ps_cs[32 * j:32 * j + 1, 0:512],
                                 rb[:, 0:1], Eb[:, j * 512:(j + 1) * 512],
                                 start=(qt == 0), stop=(qt == 15),
                                 tile_position=(0, 32 * j)).then_inc(tcl, 1)
                if h == 0:
                    t.wait_ge(oset, 1)
                t.wait_ge(stg, 16 * (h + 1))
                for tc in range(4):
                    t.wait_ge(vb, h * 4 + tc)
                    t.matmul(ps_bc[:, 0:512], ones_bf[0:1, 0:128],
                             csrow[0:1, tc * 512:(tc + 1) * 512],
                             start=True, stop=True).then_inc(tbc, 1)

        @block.scalar
        def _(s):
            for h in range(8):
                for qt in range(16):
                    n = h * 16 + qt
                    s.wait_ge(ts, 4 * (n + 1))
                    s.wait_ge(tcl, 4 * n)
                    s.activation(Eb[:, 0:S], ps_proj[:, 0:2048], AF.Exp,
                                 bias=0.0, scale=ISQ, accum_out=Zt[:]).then_inc(se, 1)

        @block.vector
        def _(v):
            for m in range(NSLAB1):
                dst = qk[m] if m < 16 else vv[m - 16]
                for tc in range(4):
                    v.wait_ge(tp, m * 4 + tc + 1)
                    v.tensor_copy(dst[:, tc * 512:(tc + 1) * 512],
                                  ps_proj[:, tc * 512:(tc + 1) * 512]).then_inc(ve, 1)
            for h in range(8):
                for qt in range(16):
                    n = h * 16 + qt
                    v.wait_ge(se, n + 1)
                    with nc.allow_low_precision("1/Z stationary is bf16"):
                        v.reciprocal(rb[:, n % 2:n % 2 + 1],
                                     Zt[:, n % 2:n % 2 + 1]).then_inc(sr, 1)
                v.wait_ge(tcl, 64 * (h + 1))
                for j in range(4):
                    cp = v.tensor_copy(cs_sb[32 * j:32 * j + 1, 0:512],
                                       ps_cs[32 * j:32 * j + 1, 0:512])
                    if j == 3:
                        cp.then_inc(vcs, 1)
                for tc in range(4):
                    v.wait_ge(tbc, h * 4 + tc + 1)
                    v.tensor_tensor(vv[h][:, tc * 512:(tc + 1) * 512],
                                    vv[h][:, tc * 512:(tc + 1) * 512],
                                    ps_bc[:, 0:512], ALU.mult).then_inc(vb, 1)

        @block.sync
        def _(sy):
            for h in range(8):
                sy.wait_ge(stg, 16 * h)
                sy.wait_ge(vcs, h + 1)
                sy.dma_start(out=csrow[0:1, :], in_=cs_sb[0:128:32, 0:512]).then_inc(stg, 16)
                sy.wait_ge(stv, 16 * h)
                sy.wait_ge(vb, 4 * (h + 1))
                sy.dma_start(out=vals_o[h], in_=vv[h][:]).then_inc(stv, 16)
            sy.wait_ge(stg, 16 * 8)
            sy.wait_ge(stv, 16 * 8)
    return nc


def build_l2():
    nc = bass.Bass("TRN2", target_bir_lowering=False)
    vals_i = nc.dram_tensor("vals16", [128, H, TT], BF16, kind="ExternalInput").ap()
    xTm = nc.dram_tensor("xTm", [128, KD, TT], F32, kind="ExternalInput").ap()
    wo = nc.dram_tensor("wo", [16, 128, KD * 128], F32, kind="ExternalInput").ap()
    w8 = nc.dram_tensor("w8", [128, KD * KD * 128], FP8, kind="ExternalInput").ap()
    s_in = nc.dram_tensor("s_in", [1, 1], BF16, kind="ExternalInput").ap()
    w_in = nc.dram_tensor("w_in", [1, 1], F32, kind="ExternalInput").ap()
    out_o = nc.dram_tensor("outT", [KD, 128, TT], F32, kind="ExternalOutput").ap()

    with ExitStack() as ctx:
        ent = ctx.enter_context
        ones_bf = ent(nc.sbuf_tensor("ones_bf", [128, 128], BF16))
        ones_f = ent(nc.sbuf_tensor("ones_f", [1, 128], F32))
        vsb = ent(nc.sbuf_tensor("vsb", [128, H * TT], BF16))
        xm = ent(nc.sbuf_tensor("xm", [128, KD * TT], BF16))
        res = ent(nc.sbuf_tensor("res", [128, KD * TT], BF16))
        w8s = ent(nc.sbuf_tensor("w8s", [128, KD * KD * 128], FP8))
        wpool = [ent(nc.sbuf_tensor(f"wp{i}", [128, KD * 128], BF16)) for i in range(4)]
        t1 = ent(nc.sbuf_tensor("t1", [128, TT], F32))
        d8t = ent(nc.sbuf_tensor("d8t", [128, TT], FP8))
        h8t = ent(nc.sbuf_tensor("h8t", [128, TT], FP8))
        hbt = ent(nc.sbuf_tensor("hbt", [128, TT], BF16))
        dbt = ent(nc.sbuf_tensor("dbt", [128, TT], BF16))
        msk = ent(nc.sbuf_tensor("msk", [128, TT], U8))
        mskf = ent(nc.sbuf_tensor("mskf", [128, TT], F32))
        stats_sb = ent(nc.sbuf_tensor("stats_sb", [128, 512], F32))
        row_sb = ent(nc.sbuf_tensor("row_sb", [1, 2048], F32))
        murow = ent(nc.sbuf_tensor("murow", [1, TT], F32))
        e2row = ent(nc.sbuf_tensor("e2row", [1, TT], F32))
        sdrow = ent(nc.sbuf_tensor("sdrow", [1, TT], F32))
        rrow = ent(nc.sbuf_tensor("rrow", [1, TT], F32))
        nrow = ent(nc.sbuf_tensor("nrow", [1, TT], F32))
        s_f = ent(nc.sbuf_tensor("s_f", [1, 1], F32))
        srow = ent(nc.sbuf_tensor("srow", [1, 1], F32))
        swrow = ent(nc.sbuf_tensor("swrow", [1, 1], F32))
        rs_sb = ent(nc.sbuf_tensor("rs_sb", [128, 1], F32))
        sw_sb = ent(nc.sbuf_tensor("sw_sb", [128, 1], F32))
        tmpn = ent(nc.sbuf_tensor("tmpn", [128, 512], F32))
        outst = [ent(nc.sbuf_tensor(f"outst{i}", [128, TT], F32)) for i in range(2)]

        ps_mm = ent(nc.psum_tensor("ps_mm", [128, 1024], F32))
        ps_st = ent(nc.psum_tensor("ps_st", [128, 512], F32))
        ps_b1 = ent(nc.psum_tensor("ps_b1", [128, 1024], F32))
        ps_b2 = ent(nc.psum_tensor("ps_b2", [128, 1024], F32))
        ps_sc = ent(nc.psum_tensor("ps_sc", [128, 2], F32))

        lbig = ent(nc.semaphore("lbig"))
        lw = [ent(nc.semaphore(f"lw{i}")) for i in range(4)]
        lds = ent(nc.semaphore("lds"))
        vpre = ent(nc.semaphore("vpre"))
        tsc = ent(nc.semaphore("tsc"))
        tmm = ent(nc.semaphore("tmm"))
        vev = ent(nc.semaphore("vev"))
        ssq = ent(nc.semaphore("ssq"))
        tst = ent(nc.semaphore("tst"))
        vst = ent(nc.semaphore("vst"))
        stg = ent(nc.semaphore("stg"))
        srw = ent(nc.semaphore("srw"))
        ssd = ent(nc.semaphore("ssd"))
        tbc = ent(nc.semaphore("tbc"))
        vn = ent(nc.semaphore("vn"))
        vq = ent(nc.semaphore("vq"))
        vrr = ent(nc.semaphore("vrr"))
        vch = ent(nc.semaphore("vch"))
        sto = ent(nc.semaphore("sto"))
        oset = ent(nc.semaphore("oset"))

        block = ent(nc.Block())

        @block.gpsimd
        def _(g):
            g.memset(ones_bf[:], 1.0).then_inc(oset, 1)
            g.memset(ones_f[:], 1.0).then_inc(oset, 1)
            g.dma_start(out=s_f[:], in_=s_in).then_inc(lds, 16)
            g.dma_start(out=swrow[:], in_=w_in).then_inc(lds, 16)
            g.dma_start(out=vsb[:], in_=vals_i).then_inc(lbig, 16)
            g.dma_start(out=xm[:], in_=xTm).then_inc(lbig, 16)
            g.dma_start(out=w8s[:], in_=w8).then_inc(lbig, 16)
            for m in range(16):
                if m >= 4:
                    g.wait_ge(lw[m % 4], 16 * (m // 4))
                    g.wait_ge(tmm, 2 * (m - 3))
                g.dma_start(out=wpool[m % 4][:], in_=wo[m]).then_inc(lw[m % 4], 16)

        @block.vector
        def _(v):
            vc = [0]
            def VI(instr):
                vc[0] += 1
                instr.then_inc(vch, 1)
            def VBAR():
                v.wait_ge(vch, vc[0])
            v.wait_ge(lds, 32)
            v.reciprocal(srow[:], s_f[:]).then_inc(vpre, 1)
            v.tensor_tensor(swrow[:], swrow[:], s_f[:], ALU.mult).then_inc(vpre, 1)
            v.wait_ge(tsc, 2)
            v.tensor_copy(rs_sb[:], ps_sc[:, 0:1]).then_inc(vpre, 1)
            v.tensor_copy(sw_sb[:], ps_sc[:, 1:2]).then_inc(vpre, 1)
            for m in range(KD):
                for tc in range(2):
                    v.wait_ge(tmm, m * 2 + tc + 1)
                    sl = slice(m * TT + tc * 512, m * TT + tc * 512 + 512)
                    v.tensor_tensor(res[:, sl], ps_mm[:, tc * 512:(tc + 1) * 512],
                                    xm[:, sl], ALU.add).then_inc(vev, 1)
            v.wait_ge(tst, 64)
            for j in range(4):
                cp = v.tensor_copy(stats_sb[32 * j:32 * j + 1, 0:512],
                                   ps_st[32 * j:32 * j + 1, 0:512])
                if j == 3:
                    cp.then_inc(vst, 1)
            v.wait_ge(stg, 16)
            VI(v.tensor_scalar(murow[:], row_sb[0:1, 0:1024], 1.0 / 2048.0, None, ALU.mult))
            VI(v.tensor_scalar(e2row[:], row_sb[0:1, 1024:2048], 1.0 / 2048.0, None, ALU.mult))
            VBAR()
            VI(v.tensor_tensor(sdrow[:], murow[:], murow[:], ALU.mult))
            VBAR()
            VI(v.tensor_tensor(sdrow[:], e2row[:], sdrow[:], ALU.subtract))
            VBAR()
            v.tensor_scalar(sdrow[:], sdrow[:], 1e-5, None, ALU.add).then_inc(srw, 1)
            v.wait_ge(ssd, 1)
            VI(v.reciprocal(rrow[:], sdrow[:]))
            VBAR()
            VI(v.tensor_tensor(nrow[:], murow[:], rrow[:], ALU.mult))
            VBAR()
            v.tensor_scalar(nrow[:], nrow[:], -1.0, None, ALU.mult).then_inc(srw, 1)
            v.wait_ge(tbc, 4)
            nadds = 0
            for tc in range(2):
                for m in range(KD):
                    sl = slice(m * TT + tc * 512, m * TT + tc * 512 + 512)
                    bs = slice(tc * 512, tc * 512 + 512)
                    if nadds:
                        v.wait_ge(vn, nadds)
                    VI(v.tensor_tensor(tmpn[:], res[:, sl], ps_b1[:, bs], ALU.mult))
                    VBAR()
                    v.tensor_tensor(xm[:, sl], tmpn[:], ps_b2[:, bs], ALU.add).then_inc(vn, 1)
                    nadds += 1
            v.wait_ge(vpre, 4)
            v.wait_ge(vn, 32)
            for m in range(KD):
                sl = slice(m * TT, (m + 1) * TT)
                VI(v.tensor_scalar(t1[:], xm[:, sl], rs_sb[:], -448.0, ALU.mult, ALU.max))
                VBAR()
                VI(v.tensor_scalar(d8t[:], t1[:], 448.0, None, ALU.min))
                VI(v.tensor_scalar(h8t[:], t1[:], 448.0, 0.5, ALU.min, ALU.mult))
                VI(v.tensor_tensor(mskf[:], t1[:], t1[:], ALU.mult))
                VBAR()
                VI(v.tensor_copy(dbt[:], d8t[:]))
                VI(v.tensor_scalar(hbt[:], h8t[:], 2.0, None, ALU.mult))
                VI(v.tensor_scalar(msk[:], mskf[:], 50176.0, None, ALU.is_ge))
                VBAR()
                v.select(vsb[:, sl], msk[:], hbt[:], dbt[:], add_drain=True).then_inc(vq, 1)
            for o in range(KD):
                for tc in range(2):
                    v.wait_ge(tmm, 32 + o * 2 + tc + 1)
                    sl = slice(o * TT + tc * 512, o * TT + tc * 512 + 512)
                    v.scalar_tensor_tensor(res[:, sl], ps_mm[:, tc * 512:(tc + 1) * 512],
                                           sw_sb[:], xm[:, sl],
                                           ALU.mult, ALU.add).then_inc(vev, 1)
            v.wait_ge(tst, 128)
            for j in range(4):
                cp = v.tensor_copy(stats_sb[32 * j:32 * j + 1, 0:512],
                                   ps_st[32 * j:32 * j + 1, 0:512])
                if j == 3:
                    cp.then_inc(vst, 1)
            v.wait_ge(stg, 32)
            VI(v.tensor_scalar(murow[:], row_sb[0:1, 0:1024], 1.0 / 2048.0, None, ALU.mult))
            VI(v.tensor_scalar(e2row[:], row_sb[0:1, 1024:2048], 1.0 / 2048.0, None, ALU.mult))
            VBAR()
            VI(v.tensor_tensor(sdrow[:], murow[:], murow[:], ALU.mult))
            VBAR()
            VI(v.tensor_tensor(sdrow[:], e2row[:], sdrow[:], ALU.subtract))
            VBAR()
            v.tensor_scalar(sdrow[:], sdrow[:], 1e-5, None, ALU.add).then_inc(srw, 1)
            v.wait_ge(ssd, 2)
            VI(v.reciprocal(rrow[:], sdrow[:]))
            VBAR()
            VI(v.tensor_tensor(nrow[:], murow[:], rrow[:], ALU.mult))
            VBAR()
            v.tensor_scalar(nrow[:], nrow[:], -1.0, None, ALU.mult).then_inc(srw, 1)
            v.wait_ge(tbc, 8)
            for m in range(KD):
                ob = outst[m % 2]
                if m >= 2:
                    v.wait_ge(sto, 16 * (m - 1))
                if m >= 1:
                    v.wait_ge(vn, 32 + m)
                for tc in range(2):
                    sl = slice(m * TT + tc * 512, m * TT + tc * 512 + 512)
                    bs = slice(tc * 512, tc * 512 + 512)
                    VI(v.tensor_tensor(tmpn[:], res[:, sl], ps_b1[:, bs], ALU.mult))
                    VBAR()
                    ad = v.tensor_tensor(ob[:, tc * 512:(tc + 1) * 512], tmpn[:],
                                         ps_b2[:, bs], ALU.add)
                    if tc == 0:
                        VI(ad)
                        VBAR()
                    else:
                        ad.then_inc(vn, 1)

        @block.tensor
        def _(t):
            t.wait_ge(oset, 2)
            t.wait_ge(vpre, 2)
            t.matmul(ps_sc[:, 0:1], ones_f[0:1, 0:128], srow[0:1, 0:1],
                     start=True, stop=True).then_inc(tsc, 1)
            t.matmul(ps_sc[:, 1:2], ones_f[0:1, 0:128], swrow[0:1, 0:1],
                     start=True, stop=True).then_inc(tsc, 1)
            t.wait_ge(lbig, 16 * 3)
            for m in range(KD):
                t.wait_ge(lw[m % 4], 16 * (m // 4 + 1))
                for tc in range(2):
                    if m >= 1:
                        t.wait_ge(vev, 2 * (m - 1) + tc + 1)
                    for kh in range(H):
                        mm = t.matmul(ps_mm[:, tc * 512:(tc + 1) * 512],
                                      wpool[m % 4][:, kh * 128:(kh + 1) * 128],
                                      vsb[:, kh * TT + tc * 512: kh * TT + tc * 512 + 512],
                                      start=(kh == 0), stop=(kh == H - 1))
                        if kh == H - 1:
                            mm.then_inc(tmm, 1)
            t.wait_ge(vev, 32)
            t.wait_ge(ssq, 16)
            for m in range(KD):
                for j in range(4):
                    src = res if j < 2 else vsb
                    tc = j % 2
                    sl = slice(m * TT + tc * 512, m * TT + tc * 512 + 512)
                    t.matmul(ps_st[32 * j:32 * j + 1, 0:512],
                             ones_bf[0:128, 0:1], src[:, sl],
                             start=(m == 0), stop=(m == KD - 1),
                             tile_position=(0, 32 * j)).then_inc(tst, 1)
            t.wait_ge(srw, 2)
            for tc in range(2):
                bs = slice(tc * 512, tc * 512 + 512)
                t.matmul(ps_b1[:, bs], ones_f[0:1, 0:128],
                         rrow[0:1, tc * 512:(tc + 1) * 512], start=True, stop=True).then_inc(tbc, 1)
                t.matmul(ps_b2[:, bs], ones_f[0:1, 0:128],
                         nrow[0:1, tc * 512:(tc + 1) * 512], start=True, stop=True).then_inc(tbc, 1)
            for o in range(KD):
                for tc in range(2):
                    t.wait_ge(vev, 32 + (2 * (o - 1) + tc + 1 if o >= 1 else 0))
                    for ki in range(KD):
                        t.wait_ge(vq, ki + 1)
                        mm = t.matmul(ps_mm[:, tc * 512:(tc + 1) * 512],
                                      w8s[:, (ki * KD + o) * 128:(ki * KD + o) * 128 + 128],
                                      vsb[:, ki * TT + tc * 512: ki * TT + tc * 512 + 512],
                                      start=(ki == 0), stop=(ki == KD - 1))
                        if ki == KD - 1:
                            mm.then_inc(tmm, 1)
            t.wait_ge(vev, 64)
            t.wait_ge(ssq, 32)
            for m in range(KD):
                for j in range(4):
                    src = res if j < 2 else vsb
                    tc = j % 2
                    sl = slice(m * TT + tc * 512, m * TT + tc * 512 + 512)
                    t.matmul(ps_st[32 * j:32 * j + 1, 0:512],
                             ones_bf[0:128, 0:1], src[:, sl],
                             start=(m == 0), stop=(m == KD - 1),
                             tile_position=(0, 32 * j)).then_inc(tst, 1)
            t.wait_ge(srw, 4)
            t.wait_ge(vn, 32)
            for tc in range(2):
                bs = slice(tc * 512, tc * 512 + 512)
                t.matmul(ps_b1[:, bs], ones_f[0:1, 0:128],
                         rrow[0:1, tc * 512:(tc + 1) * 512], start=True, stop=True).then_inc(tbc, 1)
                t.matmul(ps_b2[:, bs], ones_f[0:1, 0:128],
                         nrow[0:1, tc * 512:(tc + 1) * 512], start=True, stop=True).then_inc(tbc, 1)

        @block.scalar
        def _(s):
            s.wait_ge(vev, 32)
            for m in range(KD):
                sl = slice(m * TT, (m + 1) * TT)
                s.activation(vsb[:, sl], res[:, sl], AF.Square,
                             bias=0.0, scale=1.0).then_inc(ssq, 1)
            s.wait_ge(srw, 1)
            s.activation(sdrow[:], sdrow[:], AF.Sqrt, bias=0.0, scale=1.0).then_inc(ssd, 1)
            s.wait_ge(vev, 64)
            for m in range(KD):
                sl = slice(m * TT, (m + 1) * TT)
                s.activation(vsb[:, sl], res[:, sl], AF.Square,
                             bias=0.0, scale=1.0).then_inc(ssq, 1)
            s.wait_ge(srw, 3)
            s.activation(sdrow[:], sdrow[:], AF.Sqrt, bias=0.0, scale=1.0).then_inc(ssd, 1)

        @block.sync
        def _(sy):
            sy.wait_ge(vst, 1)
            sy.dma_start(out=row_sb[0:1, :], in_=stats_sb[0:128:32, 0:512]).then_inc(stg, 16)
            sy.wait_ge(stg, 16)
            sy.wait_ge(vst, 2)
            sy.dma_start(out=row_sb[0:1, :], in_=stats_sb[0:128:32, 0:512]).then_inc(stg, 16)
            for m in range(KD):
                sy.wait_ge(sto, 16 * m)
                sy.wait_ge(vn, 32 + m + 1)
                sy.dma_start(out=out_o[m], in_=outst[m % 2][:]).then_inc(sto, 16)
            sy.wait_ge(sto, 16 * KD)
    return nc




NH = 16
CT = 512


def build_fused():
    nc = bass.Bass("TRN2", target_bir_lowering=False)
    xT = nc.dram_tensor("xT", [128, KD, S], F32, kind="ExternalInput").ap()
    wqk = nc.dram_tensor("wqk", [2 * NH, 128, KD * 128], F32, kind="ExternalInput").ap()
    wv = nc.dram_tensor("wv", [NH, 128, KD * 128], F32, kind="ExternalInput").ap()
    wo = nc.dram_tensor("wo", [KD, 128, KD * 128], F32, kind="ExternalInput").ap()
    w8o = nc.dram_tensor("w8o", [KD, 128, KD * 128], FP8, kind="ExternalInput").ap()
    s_in = nc.dram_tensor("s_in", [1, 1], BF16, kind="ExternalInput").ap()
    w_in = nc.dram_tensor("w_in", [1, 1], F32, kind="ExternalInput").ap()
    out_o = nc.dram_tensor("outT", [KD, 128, TT], F32, kind="ExternalOutput").ap()

    with ExitStack() as ctx:
        ent = ctx.enter_context
        ones_bf = ent(nc.sbuf_tensor("ones_bf", [128, 128], BF16))
        ones_f = ent(nc.sbuf_tensor("ones_f", [1, 128], F32))
        xs = ent(nc.sbuf_tensor("xs", [128, KD * S], BF16))
        qkp = [ent(nc.sbuf_tensor(f"qkp{i}", [128, S], BF16)) for i in range(4)]
        wpool = [ent(nc.sbuf_tensor(f"wp{i}", [128, KD * 128], BF16)) for i in range(4)]
        w8p = [ent(nc.sbuf_tensor(f"w8p{i}", [128, KD * 128], FP8)) for i in range(4)]
        Ebs = [ent(nc.sbuf_tensor(f"Eb{i}", [128, S], BF16)) for i in range(2)]
        Zt = ent(nc.sbuf_tensor("Zt", [128, 4], F32))
        ztmp = ent(nc.sbuf_tensor("ztmp", [128, 2], F32))
        rb = ent(nc.sbuf_tensor("rb", [128, 2], BF16))
        cs_sb = ent(nc.sbuf_tensor("cs_sb", [128, 512], BF16))
        cs16 = ent(nc.sbuf_tensor("cs16", [NH, 2 * CT], BF16))
        csrow = ent(nc.sbuf_tensor("csrow", [1, NH * CT], BF16))
        vv = [ent(nc.sbuf_tensor(f"vv{m}", [128, CT], BF16)) for m in range(KD)]
        res = ent(nc.sbuf_tensor("res", [128, KD * CT], BF16))
        t1 = [ent(nc.sbuf_tensor(f"t1_{i}", [128, CT], F32)) for i in range(2)]
        d8t = [ent(nc.sbuf_tensor(f"d8t{i}", [128, CT], FP8)) for i in range(2)]
        h8t = [ent(nc.sbuf_tensor(f"h8t{i}", [128, CT], FP8)) for i in range(2)]
        hbt = [ent(nc.sbuf_tensor(f"hbt{i}", [128, CT], BF16)) for i in range(2)]
        dbt = [ent(nc.sbuf_tensor(f"dbt{i}", [128, CT], BF16)) for i in range(2)]
        msk = [ent(nc.sbuf_tensor(f"msk{i}", [128, CT], U8)) for i in range(2)]
        mskf = [ent(nc.sbuf_tensor(f"mskf{i}", [128, CT], F32)) for i in range(2)]
        tmpn = [ent(nc.sbuf_tensor(f"tmpn{i}", [128, CT], F32)) for i in range(2)]
        outst = [ent(nc.sbuf_tensor(f"outst{i}", [128, CT], F32)) for i in range(4)]
        stats_sb = ent(nc.sbuf_tensor("stats_sb", [128, 512], F32))
        row_sb = ent(nc.sbuf_tensor("row_sb", [1, 2 * CT], F32))
        murow = ent(nc.sbuf_tensor("murow", [1, CT], F32))
        e2row = ent(nc.sbuf_tensor("e2row", [1, CT], F32))
        sdrow = ent(nc.sbuf_tensor("sdrow", [1, CT], F32))
        rrow = ent(nc.sbuf_tensor("rrow", [1, CT], F32))
        nrow = ent(nc.sbuf_tensor("nrow", [1, CT], F32))
        s_f = ent(nc.sbuf_tensor("s_f", [1, 1], F32))
        srow = ent(nc.sbuf_tensor("srow", [1, 1], F32))
        swrow = ent(nc.sbuf_tensor("swrow", [1, 1], F32))
        rs_sb = ent(nc.sbuf_tensor("rs_sb", [128, 1], F32))
        sw_sb = ent(nc.sbuf_tensor("sw_sb", [128, 1], F32))

        ps_acc = ent(nc.psum_tensor("ps_acc", [128, 1024], F32))
        ps_sc = ent(nc.psum_tensor("ps_sc", [128, 2048], F32))
        ps_cs = ent(nc.psum_tensor("ps_cs", [128, 512], F32))
        ps_bb = ent(nc.psum_tensor("ps_bb", [128, 2], F32))

        ldx = ent(nc.semaphore("ldx"))
        ldx2 = ent(nc.semaphore("ldx2"))
        lds = ent(nc.semaphore("lds"))
        lw = [ent(nc.semaphore(f"lw{i}")) for i in range(4)]
        lw8 = [ent(nc.semaphore(f"lw8{i}")) for i in range(4)]
        oset = ent(nc.semaphore("oset"))
        vpre = ent(nc.semaphore("vpre"))
        tsc = ent(nc.semaphore("tsc"))
        tp = ent(nc.semaphore("tp"))
        cve = ent(nc.semaphore("cve"))
        ts = ent(nc.semaphore("ts"))
        se = ent(nc.semaphore("se"))
        sr = ent(nc.semaphore("sr"))
        tcl = ent(nc.semaphore("tcl"))
        vcs = ent(nc.semaphore("vcs"))
        stgh = ent(nc.semaphore("stgh"))
        stgc = ent(nc.semaphore("stgc"))
        tbc = ent(nc.semaphore("tbc"))
        vmu = ent(nc.semaphore("vmu"))
        tst = ent(nc.semaphore("tst"))
        vst = ent(nc.semaphore("vst"))
        stgr = ent(nc.semaphore("stgr"))
        srw = ent(nc.semaphore("srw"))
        ssd = ent(nc.semaphore("ssd"))
        ssq = ent(nc.semaphore("ssq"))
        vn = ent(nc.semaphore("vn"))
        vq = ent(nc.semaphore("vq"))
        vqd = ent(nc.semaphore("vqd"))
        sq8 = ent(nc.semaphore("sq8"))
        stp = [ent(nc.semaphore("sto0")), ent(nc.semaphore("sto1"))]
        vch = ent(nc.semaphore("vch"))
        block = ent(nc.Block())

        qk_groups = [("qk", s, tc) for s in range(2 * NH) for tc in range(4)]
        sched = {}
        sched[-1] = [(g, hf) for g in range(8) for hf in (0, 1)]
        gidx = 8
        for h in range(NH):
            for qt in range(16):
                sched[(h, qt)] = []
            if h < NH - 1:
                for qt in range(16):
                    sched[(h, qt)] = [(gidx + qt // 2, qt % 2)]
                gidx += 8
        assert gidx == 128
        for qt in range(16):
            sched[(15, qt)] = [(128 + 2 * qt, 0), (128 + 2 * qt, 1)] if False else \
                              [(128 + qt, 0), (128 + qt, 1)]

        GTAB = [("v", 0), ("wo", 0), ("v", 1), ("wo", 1), ("ffn", 0), ("ffn", 1)]

        def group_info(g):
            pg = g - 128
            kind, ch = GTAB[pg // 16]
            return kind, ch, pg % 16

        def slab_of_group(g):
            if g < 128:
                return g // 4
            kind, ch, idx = group_info(g)
            if kind == "v":
                return 32 + ch * 32 + idx
            if kind == "wo":
                return 48 + ch * 32 + idx
            return None
        n_groups_total = 128 + 6 * 16
        group_end_of_slab = {}
        for g in range(n_groups_total):
            sl_ = slab_of_group(g)
            if sl_ is not None:
                group_end_of_slab[sl_] = g + 1

        @block.gpsimd
        def _(g_):
            g_.memset(ones_bf[:], 1.0).then_inc(oset, 1)
            g_.memset(ones_f[:], 1.0).then_inc(oset, 1)
            g_.dma_start(out=s_f[:], in_=s_in).then_inc(lds, 16)
            g_.dma_start(out=swrow[:], in_=w_in).then_inc(lds, 16)
            g_.dma_start(out=xs[:, 0:8 * S], in_=xT[:, 0:8, :]).then_inc(ldx, 16)
            g_.dma_start(out=xs[:, 8 * S:KD * S], in_=xT[:, 8:KD, :]).then_inc(ldx2, 16)
            stream = [("w", s, wqk[s]) for s in range(2 * NH)]
            for ch in range(2):
                stream += [("w", 32 + ch * 32 + i, wv[i]) for i in range(NH)]
                stream += [("w", 48 + ch * 32 + i, wo[i]) for i in range(KD)]
            stream += [("8", j, w8o[j % KD]) for j in range(2 * KD)]
            for kind, s, src_ap in stream:
                if kind == "w":
                    if s >= 4:
                        g_.wait_ge(lw[s % 4], 16 * (s // 4))
                        g_.wait_ge(tp, group_end_of_slab[s - 4])
                    g_.dma_start(out=wpool[s % 4][:], in_=src_ap).then_inc(lw[s % 4], 16)
                else:
                    if s >= 4:
                        g_.wait_ge(lw8[s % 4], 16 * (s // 4))
                        chp, op = divmod(s - 4, KD)
                        g_.wait_ge(tp, 192 + chp * 16 + op + 1)
                    g_.dma_start(out=w8p[s % 4][:], in_=src_ap).then_inc(lw8[s % 4], 16)

        vvB = [qkp[r // 4][:, (r % 4) * 512:(r % 4) * 512 + 512] for r in range(KD)]
        resB = [xs[:, m * S + S - 512: m * S + S] for m in range(KD)]
        sqB = [xs[:, m * S + 2 * CT: m * S + 3 * CT] for m in range(KD)]

        @block.tensor
        def _(t):
            t.wait_ge(oset, 2)
            t.wait_ge(vpre, 2)
            t.matmul(ps_bb[:, 0:1], ones_f[0:1, 0:128], srow[0:1, 0:1],
                     start=True, stop=True).then_inc(tsc, 1)
            t.matmul(ps_bb[:, 1:2], ones_f[0:1, 0:128], swrow[0:1, 0:1],
                     start=True, stop=True).then_inc(tsc, 1)

            def emit_group_half(g, hf):
                bank = g % 2
                osl = slice(bank * 512, bank * 512 + 512)
                kds = range(8 * hf, 8 * hf + 8)
                if hf == 0:
                    if g >= 2:
                        t.wait_ge(cve, g - 1)
                if g < 128:
                    s, tc = g // 4, g % 4
                    if hf == 0:
                        t.wait_ge(lw[s % 4], 16 * (s // 4 + 1))
                    if g < 8:
                        t.wait_ge(ldx if hf == 0 else ldx2, 16)
                    for kd in kds:
                        mm = t.matmul(ps_acc[:, osl], wpool[s % 4][:, kd * 128:(kd + 1) * 128],
                                      xs[:, kd * S + tc * 512: kd * S + tc * 512 + 512],
                                      start=(kd == 0), stop=(kd == KD - 1),
                                      skip_group_check=True)
                        if kd == KD - 1:
                            mm.then_inc(tp, 1)
                    return
                kind, ch, idx = group_info(g)
                if kind == "v":
                    s = 32 + ch * 32 + idx
                    if hf == 0:
                        t.wait_ge(lw[s % 4], 16 * (s // 4 + 1))
                    for kd in kds:
                        mm = t.matmul(ps_acc[:, osl], wpool[s % 4][:, kd * 128:(kd + 1) * 128],
                                      xs[:, kd * S + ch * CT: kd * S + ch * CT + CT],
                                      start=(kd == 0), stop=(kd == KD - 1),
                                      skip_group_check=True)
                        if kd == KD - 1:
                            mm.then_inc(tp, 1)
                    return
                if kind == "wo":
                    s = 48 + ch * 32 + idx
                    vsrc = vv if ch == 0 else vvB
                    if hf == 0:
                        t.wait_ge(lw[s % 4], 16 * (s // 4 + 1))
                        t.wait_ge(vmu, (ch + 1) * NH)
                    for kh in kds:
                        mm = t.matmul(ps_acc[:, osl], wpool[s % 4][:, kh * 128:(kh + 1) * 128],
                                      vsrc[kh][:, 0:CT] if ch == 0 else vsrc[kh],
                                      start=(kh == 0), stop=(kh == NH - 1),
                                      skip_group_check=True)
                        if kh == NH - 1:
                            mm.then_inc(tp, 1)
                    return
                j = ch * KD + idx
                if hf == 0:
                    t.wait_ge(lw8[j % 4], 16 * (j // 4 + 1))
                    t.wait_ge(vq, (ch + 1) * KD)
                for ki in kds:
                    mm = t.matmul(ps_acc[:, osl], w8p[j % 4][:, ki * 128:(ki + 1) * 128],
                                  vv[ki][:, 0:CT],
                                  start=(ki == 0), stop=(ki == KD - 1),
                                  skip_group_check=True)
                    if ki == KD - 1:
                        mm.then_inc(tp, 1)

            def emit_group(g):
                emit_group_half(g, 0)
                emit_group_half(g, 1)

            ng = 0
            for g, hf in sched[-1]:
                emit_group_half(g, hf)
            def emit_colsum(h, m):
                n = h * 16 + m
                t.wait_ge(sr, n + 1)
                if m == 0:
                    t.wait_ge(vcs, h)
                for j in range(4):
                    t.matmul(ps_cs[32 * j:32 * j + 1, 0:256],
                             rb[:, m % 2:m % 2 + 1], Ebs[m % 2][:, j * 256:(j + 1) * 256],
                             start=(m == 0), stop=(m == 15),
                             tile_position=(0, 32 * j)).then_inc(tcl, 1)

            for h in range(NH):
                for qt in range(16):
                    n = h * 16 + qt
                    if qt == 0:
                        t.wait_ge(cve, 8 * (h + 1))
                    t.wait_ge(se, max(0, 2 * n - 1))
                    for kc in range(2):
                        t.matmul(ps_sc[:, kc * 512:(kc + 1) * 512],
                                 qkp[(2 * h) % 4][:, qt * 128:(qt + 1) * 128],
                                 qkp[(2 * h + 1) % 4][:, kc * 512:(kc + 1) * 512],
                                 start=True, stop=True).then_inc(ts, 1)
                    for g, hf in sched[(h, qt)]:
                        emit_group_half(g, hf)
                    t.wait_ge(se, 2 * n)
                    for kc in range(2, 4):
                        t.matmul(ps_sc[:, kc * 512:(kc + 1) * 512],
                                 qkp[(2 * h) % 4][:, qt * 128:(qt + 1) * 128],
                                 qkp[(2 * h + 1) % 4][:, kc * 512:(kc + 1) * 512],
                                 start=True, stop=True).then_inc(ts, 1)
                    if qt >= 1:
                        emit_colsum(h, qt - 1)
                emit_colsum(h, 15)
            t.wait_ge(stgc, 16)
            t.wait_ge(se, 512)
            for hh in range(NH):
                bs = slice((hh % 2) * 512, (hh % 2) * 512 + 512)
                if hh >= 2:
                    t.wait_ge(vmu, hh - 1)
                t.matmul(ps_sc[:, bs], ones_bf[0:1, 0:128],
                         csrow[0:1, hh * CT:(hh + 1) * CT],
                         start=True, stop=True).then_inc(tbc, 1)
            for i in range(16):
                emit_group(144 + i)
            for m in range(KD):
                t.wait_ge(cve, 144 + m + 1)
                t.wait_ge(ssq, m + 1)
                t.matmul(ps_cs[0:1, 0:512], ones_bf[0:128, 0:1],
                         res[:, m * CT:(m + 1) * CT],
                         start=(m == 0), stop=(m == KD - 1),
                         tile_position=(0, 0)).then_inc(tst, 1)
                t.matmul(ps_cs[32:33, 0:512], ones_bf[0:128, 0:1],
                         vv[m][:, 0:CT],
                         start=(m == 0), stop=(m == KD - 1),
                         tile_position=(0, 32)).then_inc(tst, 1)
            t.wait_ge(srw, 2)
            t.matmul(ps_sc[:, 512:1024], ones_f[0:1, 0:128], rrow[0:1, 0:CT],
                     start=True, stop=True).then_inc(tbc, 1)
            t.matmul(ps_sc[:, 1024:1536], ones_f[0:1, 0:128], nrow[0:1, 0:CT],
                     start=True, stop=True).then_inc(tbc, 1)
            for i in range(16):
                emit_group(160 + i)
            t.wait_ge(stgc, 32)
            for hh in range(NH):
                bs = slice(0, 512) if hh % 2 == 0 else slice(1536, 2048)
                t.wait_ge(vmu, NH + hh - 1)
                t.matmul(ps_sc[:, bs], ones_bf[0:1, 0:128],
                         csrow[0:1, hh * CT:(hh + 1) * CT],
                         start=True, stop=True).then_inc(tbc, 1)
            for i in range(16):
                emit_group(176 + i)
            for i in range(16):
                emit_group(192 + i)
            t.wait_ge(vst, 1)
            for m in range(KD):
                t.wait_ge(cve, 192 + m + 1)
                t.wait_ge(ssq, 2 * KD + m + 1)
                t.matmul(ps_cs[0:1, 0:512], ones_bf[0:128, 0:1],
                         res[:, m * CT:(m + 1) * CT],
                         start=(m == 0), stop=(m == KD - 1),
                         tile_position=(0, 0)).then_inc(tst, 1)
                t.matmul(ps_cs[32:33, 0:512], ones_bf[0:128, 0:1],
                         vv[m][:, 0:CT],
                         start=(m == 0), stop=(m == KD - 1),
                         tile_position=(0, 32)).then_inc(tst, 1)
            t.wait_ge(srw, 4)
            t.wait_ge(vn, KD)
            t.matmul(ps_sc[:, 512:1024], ones_f[0:1, 0:128], rrow[0:1, 0:CT],
                     start=True, stop=True).then_inc(tbc, 1)
            t.matmul(ps_sc[:, 1024:1536], ones_f[0:1, 0:128], nrow[0:1, 0:CT],
                     start=True, stop=True).then_inc(tbc, 1)
            t.wait_ge(vst, 2)
            for m in range(KD):
                t.wait_ge(cve, 176 + m + 1)
                t.wait_ge(ssq, KD + m + 1)
                t.matmul(ps_cs[0:1, 0:512], ones_bf[0:128, 0:1],
                         resB[m], start=(m == 0), stop=(m == KD - 1),
                         tile_position=(0, 0)).then_inc(tst, 1)
                t.matmul(ps_cs[32:33, 0:512], ones_bf[0:128, 0:1],
                         sqB[m], start=(m == 0), stop=(m == KD - 1),
                         tile_position=(0, 32)).then_inc(tst, 1)
            t.wait_ge(srw, 6)
            t.wait_ge(vmu, 2 * NH)
            t.matmul(ps_sc[:, 0:512], ones_f[0:1, 0:128], rrow[0:1, 0:CT],
                     start=True, stop=True).then_inc(tbc, 1)
            t.matmul(ps_sc[:, 1536:2048], ones_f[0:1, 0:128], nrow[0:1, 0:CT],
                     start=True, stop=True).then_inc(tbc, 1)
            for i in range(16):
                emit_group(208 + i)
            t.wait_ge(vst, 3)
            for m in range(KD):
                t.wait_ge(cve, 208 + m + 1)
                t.wait_ge(ssq, 3 * KD + m + 1)
                t.matmul(ps_cs[0:1, 0:512], ones_bf[0:128, 0:1],
                         res[:, m * CT:(m + 1) * CT],
                         start=(m == 0), stop=(m == KD - 1),
                         tile_position=(0, 0)).then_inc(tst, 1)
                t.matmul(ps_cs[32:33, 0:512], ones_bf[0:128, 0:1],
                         vv[m][:, 0:CT],
                         start=(m == 0), stop=(m == KD - 1),
                         tile_position=(0, 32)).then_inc(tst, 1)
            t.wait_ge(srw, 8)
            t.wait_ge(vn, 3 * KD)
            t.matmul(ps_sc[:, 512:1024], ones_f[0:1, 0:128], rrow[0:1, 0:CT],
                     start=True, stop=True).then_inc(tbc, 1)
            t.matmul(ps_sc[:, 1024:1536], ones_f[0:1, 0:128], nrow[0:1, 0:CT],
                     start=True, stop=True).then_inc(tbc, 1)

        @block.scalar
        def _(s_):
            for h in range(NH):
                for qt in range(16):
                    n = h * 16 + qt
                    s_.wait_ge(ts, 4 * n + 2)
                    if n >= 1:
                        s_.wait_ge(tcl, 4 * (n - 1))
                    p = n % 2
                    s_.activation(Ebs[p][:, 0:1024], ps_sc[:, 0:1024], AF.Exp,
                                  bias=0.0, scale=ISQ,
                                  accum_out=Zt[:, 2 * p:2 * p + 1]).then_inc(se, 1)
                    s_.wait_ge(ts, 4 * n + 4)
                    s_.activation(Ebs[p][:, 1024:2048], ps_sc[:, 1024:2048], AF.Exp,
                                  bias=0.0, scale=ISQ,
                                  accum_out=Zt[:, 2 * p + 1:2 * p + 2]).then_inc(se, 1)
            def qdq_helpers(ch):
                for p in range(KD // 2):
                    pp = ch * KD // 2 + p
                    if pp >= 1:
                        s_.wait_ge(vq, 2 * pp)
                    for j in range(2):
                        s_.wait_ge(vqd, 4 * pp + 2 * j + 2)
                        s_.activation(d8t[j][:], t1[j][:], AF.Copy,
                                      bias=0.0, scale=1.0).then_inc(sq8, 1)
                        s_.wait_ge(sq8, 8 * pp + 4 * j + 1)
                        s_.activation(dbt[j][:], d8t[j][:], AF.Copy,
                                      bias=0.0, scale=1.0).then_inc(sq8, 1)
                        s_.activation(hbt[j][:], h8t[j][:], AF.Copy,
                                      bias=0.0, scale=2.0).then_inc(sq8, 1)
                        s_.activation(mskf[j][:], h8t[j][:], AF.Square,
                                      bias=0.0, scale=2.0).then_inc(sq8, 1)

            def squares(cbase, srcs, dsts, tpw=0, tstw=0):
                if tpw:
                    s_.wait_ge(tp, tpw)
                if tstw:
                    s_.wait_ge(tst, tstw)
                for m in range(KD):
                    s_.wait_ge(cve, cbase + m + 1)
                    s_.activation(dsts(m), srcs(m), AF.Square,
                                  bias=0.0, scale=1.0).then_inc(ssq, 1)

            def sqrt_stage(k):
                s_.wait_ge(srw, 2 * k + 1)
                s_.activation(sdrow[:], sdrow[:], AF.Sqrt,
                              bias=0.0, scale=1.0).then_inc(ssd, 1)

            res_m = lambda m: res[:, m * CT:(m + 1) * CT]
            vv_m = lambda m: vv[m][:, 0:CT]
            squares(144, res_m, vv_m, tpw=160)
            sqrt_stage(0)
            qdq_helpers(0)
            squares(176, lambda m: resB[m], lambda m: sqB[m])
            squares(192, res_m, vv_m, tpw=208, tstw=32)
            sqrt_stage(1)
            sqrt_stage(2)
            qdq_helpers(1)
            squares(208, res_m, vv_m, tpw=224, tstw=96)
            sqrt_stage(3)

        @block.vector
        def _(v):
            vc = [0]
            def VI(instr):
                vc[0] += 1
                instr.then_inc(vch, 1)
            def VBAR():
                v.wait_ge(vch, vc[0])
            v.wait_ge(lds, 32)
            v.reciprocal(srow[:], s_f[:]).then_inc(vpre, 1)
            v.tensor_tensor(swrow[:], swrow[:], s_f[:], ALU.mult).then_inc(vpre, 1)
            v.wait_ge(tsc, 2)
            v.tensor_copy(rs_sb[:], ps_bb[:, 0:1]).then_inc(vpre, 1)
            v.tensor_copy(sw_sb[:], ps_bb[:, 1:2]).then_inc(vpre, 1)

            def consume_group(g):
                bank = g % 2
                osl = slice(bank * 512, bank * 512 + 512)
                v.wait_ge(tp, g + 1)
                if g < 128:
                    s, tc = g // 4, g % 4
                    if s >= 4:
                        v.wait_ge(ts, 64 * ((s - 4) // 2 + 1))
                    v.tensor_copy(qkp[s % 4][:, tc * 512:(tc + 1) * 512],
                                  ps_acc[:, osl]).then_inc(cve, 1)
                    return
                kind, ch, idx = group_info(g)
                if kind == "v":
                    dst = vv[idx][:, 0:CT] if ch == 0 else vvB[idx]
                    v.tensor_copy(dst, ps_acc[:, osl]).then_inc(cve, 1)
                    return
                if kind == "wo":
                    m = idx
                    sl = slice(m * S + ch * CT, m * S + ch * CT + CT)
                    dst = res[:, m * CT:(m + 1) * CT] if ch == 0 else resB[m]
                    v.tensor_tensor(dst, ps_acc[:, osl],
                                    xs[:, sl], ALU.add).then_inc(cve, 1)
                    return
                o = idx
                xnsl = slice(o * S + ch * CT, o * S + ch * CT + CT)
                v.scalar_tensor_tensor(res[:, o * CT:(o + 1) * CT], ps_acc[:, osl],
                                       sw_sb[:], xs[:, xnsl],
                                       ALU.mult, ALU.add).then_inc(cve, 1)

            for g, hf in sched[-1]:
                if hf == 1:
                    consume_group(g)
            for h in range(NH):
                for qt in range(16):
                    n = h * 16 + qt
                    for g, hf in sched[(h, qt)]:
                        if hf == 1:
                            consume_group(g)
                    v.wait_ge(se, 2 * (n + 1))
                    p = n % 2
                    VI(v.tensor_tensor(ztmp[:, p:p + 1], Zt[:, 2 * p:2 * p + 1],
                                       Zt[:, 2 * p + 1:2 * p + 2], ALU.add))
                    VBAR()
                    with nc.allow_low_precision("1/Z stationary is bf16"):
                        v.reciprocal(rb[:, p:p + 1],
                                     ztmp[:, p:p + 1]).then_inc(sr, 1)
                v.wait_ge(tcl, 64 * (h + 1))
                v.wait_ge(stgh, 16 * h)
                for j in range(4):
                    cp = v.tensor_copy(cs_sb[32 * j:32 * j + 1, 0:256],
                                       ps_cs[32 * j:32 * j + 1, 0:256])
                if True:
                    cp.then_inc(vcs, 1)
            def rows_stage(k, tbc_guard=0):
                v.wait_ge(tst, 32 * (k + 1))
                v.tensor_copy(stats_sb[0:1, 0:512], ps_cs[0:1, 0:512])
                v.tensor_copy(stats_sb[32:33, 0:512], ps_cs[32:33, 0:512]).then_inc(vst, 1)
                if tbc_guard:
                    v.wait_ge(tbc, tbc_guard)
                v.wait_ge(stgr, 16 * (k + 1))
                VI(v.tensor_scalar(murow[:], row_sb[0:1, 0:CT], 1.0 / 2048.0, None, ALU.mult))
                VI(v.tensor_scalar(e2row[:], row_sb[0:1, CT:2 * CT], 1.0 / 2048.0, None, ALU.mult))
                VBAR()
                VI(v.tensor_tensor(sdrow[:], murow[:], murow[:], ALU.mult))
                VBAR()
                VI(v.tensor_tensor(sdrow[:], e2row[:], sdrow[:], ALU.subtract))
                VBAR()
                v.tensor_scalar(sdrow[:], sdrow[:], 1e-5, None, ALU.add).then_inc(srw, 1)
                v.wait_ge(ssd, k + 1)
                VI(v.reciprocal(rrow[:], sdrow[:]))
                VBAR()
                VI(v.tensor_tensor(nrow[:], murow[:], rrow[:], ALU.mult))
                VBAR()
                v.tensor_scalar(nrow[:], nrow[:], -1.0, None, ALU.mult).then_inc(srw, 1)

            def normalize(srcs, dsts, tbc_need, vn_base, rsl=slice(512, 1024),
                          nsl=slice(1024, 1536)):
                v.wait_ge(tbc, tbc_need)
                for m in range(KD):
                    tb = tmpn[m % 2]
                    if m >= 2:
                        v.wait_ge(vn, vn_base + m - 1)
                    VI(v.tensor_tensor(tb[:], srcs(m), ps_sc[:, rsl], ALU.mult))
                    VBAR()
                    v.tensor_tensor(dsts(m), tb[:], ps_sc[:, nsl],
                                    ALU.add).then_inc(vn, 1)

            def qdq(ch, vn_need, after_pair=None):
                v.wait_ge(vpre, 4)
                v.wait_ge(vn, vn_need)
                for p in range(KD // 2):
                    pp = ch * KD // 2 + p
                    if pp >= 1:
                        v.wait_ge(sq8, 8 * pp)
                    for j in range(2):
                        m = 2 * p + j
                        sl = slice(m * S + ch * CT, m * S + ch * CT + CT)
                        v.tensor_scalar(t1[j][:], xs[:, sl], rs_sb[:], -448.0,
                                        ALU.mult, ALU.max).then_inc(vqd, 1)
                        v.wait_ge(vqd, 4 * pp + 2 * j + 1)
                        v.tensor_scalar(h8t[j][:], t1[j][:], 448.0, 0.5,
                                        ALU.min, ALU.mult).then_inc(vqd, 1)
                    v.wait_ge(sq8, 8 * pp + 8)
                    for j in range(2):
                        m = 2 * p + j
                        VI(v.tensor_scalar(msk[j][:], mskf[j][:], 57600.0, None, ALU.is_ge))
                        VBAR()
                        v.select(vv[m][:, 0:CT], msk[j][:], hbt[j][:], dbt[j][:],
                                 add_drain=True).then_inc(vq, 1)
                    if after_pair is not None:
                        after_pair(p)

            for hh in range(NH):
                bs = slice((hh % 2) * 512, (hh % 2) * 512 + 512)
                v.wait_ge(tbc, hh + 1)
                v.tensor_tensor(vv[hh][:, 0:CT], vv[hh][:, 0:CT],
                                ps_sc[:, bs], ALU.mult).then_inc(vmu, 1)
            for i in range(16):
                consume_group(144 + i)
            rows_stage(0)
            normalize(lambda m: res[:, m * CT:(m + 1) * CT],
                      lambda m: xs[:, m * S: m * S + CT], 18, 0)
            for i in range(16):
                consume_group(160 + i)
            for hh in range(NH):
                bs = slice(0, 512) if hh % 2 == 0 else slice(1536, 2048)
                v.wait_ge(tbc, 19 + hh)
                v.tensor_tensor(vvB[hh], vvB[hh], ps_sc[:, bs], ALU.mult).then_inc(vmu, 1)
            def _wo1_pair(p):
                consume_group(176 + 2 * p)
                consume_group(176 + 2 * p + 1)
            qdq(0, KD, after_pair=_wo1_pair)
            for i in range(16):
                consume_group(192 + i)
            rows_stage(1)
            rows_stage(2, tbc_guard=36)
            normalize(lambda m: resB[m],
                      lambda m: xs[:, m * S + CT: m * S + 2 * CT], 38, KD,
                      rsl=slice(0, 512), nsl=slice(1536, 2048))
            qdq(1, 2 * KD)
            v.wait_ge(tbc, 36)
            for m in range(KD):
                ob = outst[m % 4]
                tb = tmpn[m % 2]
                if m >= 4:
                    v.wait_ge(stp[m % 2], 16 * ((m - 4) // 2 + 1))
                if m >= 2:
                    v.wait_ge(vn, 2 * KD + m - 1)
                VI(v.tensor_tensor(tb[:], res[:, m * CT:(m + 1) * CT],
                                   ps_sc[:, 512:1024], ALU.mult))
                VBAR()
                v.tensor_tensor(ob[:, 0:CT], tb[:], ps_sc[:, 1024:1536],
                                ALU.add).then_inc(vn, 1)
                consume_group(208 + m)
            rows_stage(3)
            v.wait_ge(tbc, 40)
            for m in range(KD):
                ob = outst[m % 4]
                tb = tmpn[m % 2]
                v.wait_ge(stp[m % 2], 16 * ((KD + m - 4) // 2 + 1))
                if m >= 2:
                    v.wait_ge(vn, 3 * KD + m - 1)
                VI(v.tensor_tensor(tb[:], res[:, m * CT:(m + 1) * CT],
                                   ps_sc[:, 512:1024], ALU.mult))
                VBAR()
                v.tensor_tensor(ob[:, 0:CT], tb[:], ps_sc[:, 1024:1536],
                                ALU.add).then_inc(vn, 1)

        @block.sync
        def _(sy):
            for h in range(NH):
                sy.wait_ge(stgh, 16 * h)
                sy.wait_ge(vcs, h + 1)
                sy.dma_start(out=cs16[h:h + 1, 0:2 * CT],
                             in_=cs_sb[0:128:32, 0:256]).then_inc(stgh, 16)
            sy.wait_ge(stgh, 16 * NH)
            sy.dma_start(out=csrow[0:1, :],
                         in_=cs16[0:NH, 0:CT]).then_inc(stgc, 16)
            sy.wait_ge(stgc, 16)
            sy.wait_ge(tbc, 16)
            sy.dma_start(out=csrow[0:1, :],
                         in_=cs16[0:NH, CT:2 * CT]).then_inc(stgc, 16)
            for k in range(3):
                sy.wait_ge(stgr, 16 * k)
                sy.wait_ge(vst, k + 1)
                sy.dma_start(out=row_sb[0:1, :],
                             in_=stats_sb[0:64:32, 0:512]).then_inc(stgr, 16)
            for m in range(KD):
                sy.wait_ge(stp[m % 2], 16 * (m // 2))
                sy.wait_ge(vn, 2 * KD + m + 1)
                sy.dma_start(out=out_o[m][:, 0:CT],
                             in_=outst[m % 4][:]).then_inc(stp[m % 2], 16)
            sy.wait_ge(stgr, 48)
            sy.wait_ge(vst, 4)
            sy.dma_start(out=row_sb[0:1, :],
                         in_=stats_sb[0:64:32, 0:512]).then_inc(stgr, 16)
            for m in range(KD):
                sy.wait_ge(stp[m % 2], 16 * ((KD + m) // 2))
                sy.wait_ge(vn, 3 * KD + m + 1)
                sy.dma_start(out=out_o[m][:, CT:2 * CT],
                             in_=outst[m % 4][:]).then_inc(stp[m % 2], 16)
            sy.wait_ge(stp[0], 16 * KD)
            sy.wait_ge(stp[1], 16 * KD)
            sy.wait_ge(stgc, 32)
    return nc


_cache = {}


def _get(name, builder):
    if name not in _cache:
        _cache[name] = builder()
    return _cache[name]


FUSED = True


def kernel(L0_attn_x, L0_attn_w_qkv, L0_attn_w_o, L0_ffn_weight,
           L0_ffn_weight_scale, L0_ffn_input_scale):
    x = np.asarray(L0_attn_x)
    wqkv = np.asarray(L0_attn_w_qkv)
    w_o = np.asarray(L0_attn_w_o)
    w8 = np.asarray(L0_ffn_weight)
    wsc = np.asarray(L0_ffn_weight_scale, dtype=np.float32).reshape(1, 1)
    isc = np.asarray(L0_ffn_input_scale).reshape(1, 1)

    def tile_slabs(A):
        ncol = A.shape[1]
        return np.ascontiguousarray(
            A.reshape(KD, 128, ncol // 128, 128).transpose(2, 1, 0, 3)
        ).reshape(ncol // 128, 128, KD * 128)

    if FUSED:
        wqk32 = tile_slabs(np.concatenate(
            [c for h in range(H)
             for c in (wqkv[:, h * 128:(h + 1) * 128],
                       wqkv[:, 2048 + h * 128: 2048 + (h + 1) * 128])], axis=1))
        wv16 = tile_slabs(wqkv[:, 4096:6144])
        wo_t = tile_slabs(w_o)
        w8o = tile_slabs(np.ascontiguousarray(w8.T))
        xTr = []
        for c in range(8):
            b, g = c // 2, c % 2
            xb = x[b].T
            rot = np.concatenate([xb[:, g * TT:(g + 1) * TT],
                                  xb[:, (1 - g) * TT:(2 - g) * TT]], axis=1)
            xTr.append(np.ascontiguousarray(
                rot.reshape(KD, 128, S).transpose(1, 0, 2)))
        nc = _get("fused", build_fused)
        ins = [dict(xT=xTr[c], wqk=wqk32, wv=wv16, wo=wo_t, w8o=w8o,
                    s_in=isc, w_in=wsc) for c in range(8)]
        r = run_bass_kernel_spmd(nc, ins, core_ids=list(range(8)))
        out = np.empty((B, S, D), dtype=np.float32)
        for c in range(8):
            b, g = c // 2, c % 2
            oT = np.asarray(r.results[c]["outT"])
            out[b, g * TT:(g + 1) * TT, :] = oT.transpose(2, 0, 1).reshape(TT, D)
        return out

    xT2 = [np.ascontiguousarray(x[b].T.reshape(KD, 128, S).transpose(1, 0, 2))
           for b in range(B)]
    wqk_t = []
    wv_t = []
    for g in range(2):
        qcols = wqkv[:, g * 1024:(g + 1) * 1024]
        kcols = wqkv[:, 2048 + g * 1024: 2048 + (g + 1) * 1024]
        vcols = wqkv[:, 4096 + g * 1024: 4096 + (g + 1) * 1024]
        wqk_t.append(tile_slabs(np.concatenate([qcols, kcols], axis=1)))
        wv_t.append(tile_slabs(vcols))

    nc1 = _get("l1", build_l1)
    in1 = [dict(xT=xT2[c // 2], wqk=wqk_t[c % 2], wv=wv_t[c % 2]) for c in range(8)]
    res1 = run_bass_kernel_spmd(nc1, in1, core_ids=list(range(8)))
    vals = [np.asarray(res1.results[c]["vals"]) for c in range(8)]

    vals16 = []
    for c in range(8):
        b, g = c // 2, c % 2
        lo, hi = g * TT, (g + 1) * TT
        v16 = np.concatenate([vals[2 * b][:, :, lo:hi], vals[2 * b + 1][:, :, lo:hi]],
                             axis=0)
        vals16.append(np.ascontiguousarray(v16.transpose(1, 0, 2)))

    wo_t = tile_slabs(w_o)
    w8T = np.ascontiguousarray(
        w8.T.reshape(KD, 128, KD, 128).transpose(1, 0, 2, 3)
    ).reshape(128, KD * KD * 128)
    xTm = [np.ascontiguousarray(xT2[b][:, :, g * TT:(g + 1) * TT])
           for b in range(B) for g in range(2)]

    nc2 = _get("l2", build_l2)
    in2 = [dict(vals16=vals16[c], xTm=xTm[c], wo=wo_t, w8=w8T,
                s_in=isc, w_in=wsc) for c in range(8)]
    res2 = run_bass_kernel_spmd(nc2, in2, core_ids=list(range(8)))

    out = np.empty((B, S, D), dtype=np.float32)
    for c in range(8):
        b, g = c // 2, c % 2
        oT = np.asarray(res2.results[c]["outT"])
        out[b, g * TT:(g + 1) * TT, :] = oT.transpose(2, 0, 1).reshape(TT, D)
    return out
